# version 07
# baseline (speedup 1.0000x reference)
import numpy as np
import concourse.bass as bass
import concourse.mybir as mybir
from concourse.bass_utils import run_bass_kernel_spmd

F32 = mybir.dt.float32
I32 = mybir.dt.int32
AF = mybir.ActivationFunctionType
ALU = mybir.AluOpType
AXX = mybir.AxisListType.X

B, T, D = 4, 4096, 1024
NCORE = 8
TPC = (B * T) // NCORE
NKC = D // 128
TT = 512
NTQ = TPC // TT
EXP = 1536
NEC = EXP // 128
HID = 4096
NHC = HID // 128
M1 = float(B * T * D)
M2 = float(B * T * HID)
C995_HI = float(np.float32(0.995))
C995_LO = float(np.float64(0.995) - np.float64(np.float32(0.995)))

Q1_PLO, Q1_PHI = 1.5, 4.0
Q2_PLO, Q2_PHI = 0.25, 4.25
NPIL = 40
DENSE_W = 0.28
NDEN = 28
EXT_W = 0.06
NBIS = 9
TBIS = 8
NTOP = 16
BCW = TPC + 3


def _tile4(w):
    K, M = w.shape
    return np.ascontiguousarray(
        w.reshape(K // 128, 128, M // 128, 128).transpose(0, 2, 1, 3))


def _col8(v):
    return np.ascontiguousarray(v.reshape(-1, 128).T)


class P:
    def __init__(self, nc):
        self.nc = nc
        self.ops = {k: [] for k in ("sync", "scalar", "vector", "gpsimd", "tensor")}
        self.cnt = {}
        self.handles = {}

    def sem(self, name):
        h = self.nc.semaphore(name).__enter__()
        self.cnt[name] = 0
        self.handles[name] = h
        return h

    def emit(self, eng, fn):
        self.ops[eng].append(fn)

    def inc(self, eng, builder, sem, n=1):
        self.cnt[sem] += n
        tgt = self.cnt[sem]
        h = self.handles[sem]
        self.emit(eng, lambda e, b=builder, h=h, n=n: b(e).then_inc(h, n))
        return tgt

    def plain(self, eng, builder):
        self.emit(eng, lambda e, b=builder: b(e))

    def wait(self, eng, sem, val):
        h = self.handles[sem]
        self.emit(eng, lambda e, h=h, v=val: e.wait_ge(h, v))

    def cur(self, sem):
        return self.cnt[sem]


def build_program(limit=99):
    nc = bass.Bass("TRN2", target_bir_lowering=False, debug=False)
    dram = {}

    def din(name, shape, dt=F32):
        dram[name] = nc.dram_tensor(name, shape, dt, kind="ExternalInput").ap()
        return dram[name]

    def dout(name, shape, dt=F32):
        dram[name] = nc.dram_tensor(name, shape, dt, kind="ExternalOutput").ap()
        return dram[name]

    def dint(name, shape, dt=F32, shared=False):
        if shared:
            dram[name] = nc.dram_tensor(name, shape, dt, addr_space="Shared").ap()
        else:
            dram[name] = nc.dram_tensor(name, shape, dt).ap()
        return dram[name]

    xT = din("xT", [D, TPC])
    xh = din("xh", [128, 128])
    dwt = din("dwt", [128, 4 * NKC])
    cols = din("cols", [128, 8 * 12])
    fb_c = din("fb_c", [128, 24])
    mb1_c = din("mb1_c", [128, 32])
    percore = din("percore", [128, 4])
    ident = din("ident", [128, 128])
    ohrow = din("ohrow", [1, 8])
    pgm = din("pgm", [1, 16])
    pwT = din("pwT", [NKC, NKC, 128, 128])
    fwT = din("fwT", [NKC, 2 * NEC, 128, 128])
    dpT = din("dpT", [NEC, NKC, 128, 128])
    m1T = din("m1T", [NKC, NHC, 128, 128])
    m2T = din("m2T", [NHC, NKC, 128, 128])
    f1T = din("f1T", [NKC, NHC, 128, 128])
    f2T = din("f2T", [NHC, NKC, 128, 128])

    outp = dout("out", [D, TPC])
    dbg = dout("dbg", [1, 64])
    dbg2 = dout("dbg2", [128, 8])

    agr_i = dint("agr_i", [NCORE, EXP])
    agr_o = dint("agr_o", [NCORE, EXP], shared=True)
    xn_sp = dint("xn_sp", [D, TPC + 3])
    x1_sp = dint("x1_sp", [D, TPC])
    x2_sp = dint("x2_sp", [D, TPC])
    ao_sp = dint("ao_sp", [D, TPC])
    cv_sp = dint("cv_sp", [NEC, 2, 128, TPC])
    ge_sp = dint("ge_sp", [NHC, 128, TPC])
    m2_sp = dint("m2_sp", [NHC, 128, TPC])
    q_sp = dint("q_sp", [NHC, 128, TPC])
    ag_i = dint("ag_i", [1, EXP])
    ag_o = dint("ag_o", [NCORE, EXP])
    ar_i = dint("ar_i", [1, 2048])
    ar_o = dint("ar_o", [1, 2048], shared=True)
    arm_i = dint("arm_i", [1, 8])
    arm_o = dint("arm_o", [1, 8], shared=True)
    tb = dint("tb", [1, 4096])

    COLS = {n: i for i, n in enumerate(
        ["ln1_g", "ln1_b", "pw_b", "ln2_g", "ln2_b", "ln3_g", "ln3_b",
         "ffn_g", "ffn_b", "mb2", "dw_b", "spare"])}

    from contextlib import ExitStack
    ctx = ExitStack()
    e = ctx.enter_context

    def sbt(name, shape, dt=F32):
        return e(nc.sbuf_tensor(name, shape, dt))

    arena = sbt("arena", [128, 16384])
    cvb = sbt("cvb", [128, 4 * TT])
    wsl = sbt("wsl", [128, 2 * 16 * 128])
    bc = sbt("bc", [128, 4 * BCW])
    rowsA = sbt("rowsA", [1, BCW])
    rowsB = sbt("rowsB", [1, BCW])
    colsb = sbt("colsb", [128, 8 * 12])
    fbb = sbt("fbb", [128, 48])
    mb1b = sbt("mb1b", [128, 32])
    dwb = sbt("dwb", [128, 4 * NKC])
    pcb = sbt("pcb", [128, 4])
    identb = sbt("identb", [128, 128])
    ohrowb = sbt("ohrowb", [1, 8])
    pgmb = sbt("pgmb", [1, 16])
    onesb = sbt("onesb", [128, 130])
    t0 = sbt("t0", [128, TT])
    t1 = sbt("t1", [128, TT])
    t2 = sbt("t2", [128, TT])
    t3 = sbt("t3", [128, TT + 3])
    zer = sbt("zer", [128, TT])
    stc = sbt("stc", [128, 32])
    qbuf = sbt("qbuf", [128, 2048])
    qtmp = sbt("qtmp", [128, 4096])
    qacc = sbt("qacc", [128, 64])
    qrow = sbt("qrow", [1, 256])
    qcol = sbt("qcol", [128, 64])
    pil = sbt("pil", [128, 1024])
    msk = sbt("msk", [128, TT], I32)
    wint = sbt("wint", [128, 1024], I32)
    agb = sbt("agb", [NCORE, EXP])
    dbgs = sbt("dbgs", [1, 64])
    tr12 = sbt("tr12", [12, 128])
    h1h = sbt("h1h", [128, 24])

    pb = [e(nc.psum_tensor(f"pb{i}", [128, TT], F32)) for i in range(8)]

    p = P(nc)
    for s in ("LD", "GD", "TE", "VE", "SE", "GE", "CC", "ST"):
        p.sem(s)

    def ld(dst_ap, src_ap):
        return p.inc("sync", lambda s, d=dst_ap, r=src_ap: s.dma_start(out=d, in_=r), "LD", 16)

    def gld(dst_ap, src_ap):
        return p.inc("gpsimd", lambda s, d=dst_ap, r=src_ap: s.dma_start(out=d, in_=r), "GD", 16)

    gst = gld

    def ld_tiles(dst_base_col, src3, n):
        l = 0
        for a in range(n):
            l = ld(wsl[:, dst_base_col + a * 128:dst_base_col + (a + 1) * 128], src3[a])
        return l

    def vop(builder):
        n = p.inc("vector", builder, "VE", 1)
        p.wait("vector", "VE", n)
        return n

    def sop(builder):
        return p.inc("scalar", builder, "SE", 1)

    def gop(builder):
        n = p.inc("gpsimd", builder, "GE", 1)
        p.wait("gpsimd", "GE", n)
        return n

    def top(builder):
        return p.inc("tensor", builder, "TE", 1)

    def vred(builder):
        n = p.inc("vector", builder, "VE", 1)
        p.wait("vector", "VE", n)
        return n

    def join(*engs):
        engs = engs or ("sync", "scalar", "vector", "gpsimd", "tensor")
        marks = {}
        for eng, sem in (("sync", "LD"), ("gpsimd", "GD"), ("tensor", "TE"),
                         ("vector", "VE"), ("scalar", "SE")):
            marks[sem] = p.cur(sem)
        marks["GE"] = p.cur("GE")
        marks["ST"] = p.cur("ST")
        for eng in engs:
            for sem in ("LD", "GD", "TE", "VE", "SE", "GE", "ST"):
                if marks[sem] > 0:
                    p.wait(eng, sem, marks[sem])

    def bcast_row(dst_tile, row_ap, n):
        o = 0
        while o < n:
            m = min(TT, n - o)
            p.wait("tensor", "VE", p.cur("VE"))
            t = top(lambda tn, row=row_ap, o=o, m=m: tn.matmul(
                pb[7][:, 0:m], onesb[0:1, 2:130], row[0:1, o:o + m],
                start=True, stop=True))
            p.wait("vector", "TE", t)
            vop(lambda v, o=o, m=m, d=dst_tile: v.tensor_copy(
                out=d[:, o:o + m], in_=pb[7][:, 0:m]))
            o += m

    def pfold(dst_row, src_cols, n, op):
        if op == ALU.add:
            v = p.cur("VE")
            p.wait("tensor", "VE", v)
            t = top(lambda tn, n=n: tn.matmul(pb[7][0:1, 0:n], onesb[:, 0:1],
                                              src_cols[:, 0:n], start=True, stop=True))
            p.wait("vector", "TE", t)
            vred(lambda v, n=n, d=dst_row: v.tensor_copy(out=d, in_=pb[7][0:1, 0:n]))
        else:
            for j in range(n):
                v = p.cur("VE")
                p.wait("tensor", "VE", v)
                t = top(lambda tn, j=j: tn.transpose(
                    pb[7][0:1, 0:128], src_cols[:, j:j + 1], identb[:]))
                p.wait("vector", "TE", t)
                vred(lambda v, j=j, d=dst_row: v.reduce_max(
                    d[0:1, j:j + 1] if False else d.tensor[d.offset_unused] if False else d,
                    pb[7][0:1, 0:128], AXX) if False else v.engine_nop())
                vred(lambda v, j=j: v.tensor_reduce(
                    qtmp[0:1, 32 + j:33 + j], pb[7][0:1, 0:128], AXX, ALU.max))
            vred(lambda v, n=n, d=dst_row: v.tensor_copy(out=d, in_=qtmp[0:1, 32:32 + n]))
        return p.cur("VE")

    def allreduce(n, op, src_row, dst_row):
        n8 = (n + 7) // 8 * 8
        bi, bo = (ar_i, ar_o) if op == ALU.add else (arm_i, arm_o)
        p.wait("gpsimd", "VE", p.cur("VE"))
        g = gst(bi[0:1, 0:n], src_row)
        p.wait("gpsimd", "GD", g)
        c = p.inc("gpsimd", lambda s, n8=n8, op=op, bi=bi, bo=bo: s.collective_compute(
            "AllReduce", op, replica_groups=[list(range(NCORE))],
            ins=[bi[0:1, 0:n8]], outs=[bo[0:1, 0:n8]]), "CC", 1)
        p.wait("gpsimd", "CC", c)
        g = gld(dst_row, bo[0:1, 0:n])
        p.wait("vector", "GD", g)

    def bcs(slot, lo, m):
        return bc[:, slot * BCW + lo:slot * BCW + lo + m]

    def ccol(name, k):
        i = COLS[name]
        return colsb[:, i * 8 + k:i * 8 + k + 1]

    ld(colsb[:], cols)
    ld(fbb[:, 0:24], fb_c)
    ld(mb1b[:], mb1_c)
    ld(dwb[:], dwt)
    ld(pcb[:], percore)
    ld(identb[:], ident)
    ld(ohrowb[:], ohrow)
    ld(pgmb[:], pgm)
    la = p.cur("LD")
    for eng in ("vector", "scalar", "gpsimd", "tensor"):
        p.wait(eng, "LD", la)
    vop(lambda v: v.memset(onesb[:], 1.0))
    vop(lambda v: v.memset(zer[:], 0.0))
    vop(lambda v: v.memset(dbgs[:], 0.0))
    vop(lambda v: v.memset(qrow[:], 0.0))
    vop(lambda v: v.memset(qtmp[:], 0.0))
    vop(lambda v: v.memset(qcol[:], 0.0))
    vop(lambda v: v.memset(qacc[:], 0.0))
    vop(lambda v: v.tensor_scalar_mul(fbb[:, 24:48], fbb[:, 0:24], -1.0))
    g = gst(ar_i[0:1, 0:64], qrow[0:1, 0:64])
    g = gst(arm_i[0:1, 0:8], qrow[0:1, 0:8])
    p.wait("gpsimd", "GD", g)
    join()

    if limit < 1: return _finish(nc, p, e, ctx, outp, dbg, dbgs, t0, outp_dummy=True)
    for wi, wt in enumerate(("f1T", "f2T")):
        src = dram[wt].rearrange("a b p f -> (a b) p f")
        pbank = pb[6 + wi]
        se_mark = [0, 0]
        for t in range(0, 256, 8):
            half = (t // 8) % 2
            base = half * 2048
            p.wait("sync", "SE", se_mark[half])
            l = ld_tiles(base, src[t:t + 8], 8)
            p.wait("scalar", "LD", l)
            p.wait("scalar", "TE", p.cur("TE"))
            sab = sop(lambda sc, base=base: sc.activation(
                out=arena[:, 0:1024], in_=wsl[:, base:base + 1024], func=AF.Abs))
            se_mark[half] = sab
            p.wait("tensor", "SE", sab)
            top(lambda tn, t=t, pbank=pbank: tn.matmul(
                pbank[0:1, :], onesb[:, 0:1], arena[:, 0:512],
                start=(t == 0), stop=False))
            top(lambda tn, t=t, pbank=pbank: tn.matmul(
                pbank[0:1, :], onesb[:, 0:1], arena[:, 512:1024],
                start=False, stop=(t == 248)))
        t_ = p.cur("TE")
        p.wait("vector", "TE", t_)
        vred(lambda v, pbank=pbank: v.tensor_copy(out=qtmp[0:1, 0:512],
                                                 in_=pbank[0:1, :]))
        vred(lambda v, wi=wi: v.tensor_reduce(qrow[0:1, 40 + wi:41 + wi],
                                              qtmp[0:1, 0:512], AXX, ALU.add))
    vred(lambda v: v.tensor_copy(out=dbgs[0:1, 20:22], in_=qrow[0:1, 40:42]))
    vred(lambda v: v.tensor_copy(out=dbgs[0:1, 24:26], in_=qrow[0:1, 40:42]))
    vred(lambda v: v.tensor_scalar(qrow[0:1, 40:42], qrow[0:1, 40:42],
                                  1.0 / (HID * D), 1e-5, ALU.mult, ALU.add))
    vred(lambda v: v.reciprocal(qrow[0:1, 42:44], qrow[0:1, 40:42]))
    bcast_row(qcol[:, 50:52], qrow[0:1, 40:42], 2)
    bcast_row(qcol[:, 52:54], qrow[0:1, 42:44], 2)
    vred(lambda v: v.tensor_copy(out=dbgs[0:1, 6:8], in_=qrow[0:1, 40:42]))
    join()

    def stats_pass(src_fn, slot_mu, slot_rs, with_halo_fn=None, n=TPC):
        for tq in range(NTQ):
            for k in range(NKC):
                src_fn(k, tq)
                v = p.cur("VE")
                p.wait("scalar", "VE", v)
                s = sop(lambda s: s.square(out=t1[:], in_=t0[:]))
                p.wait("tensor", "SE", s)
                p.wait("tensor", "VE", v)
                top(lambda tn, k=k: tn.matmul(pb[6][0:1, :], onesb[:, 0:1], t0[:],
                                              start=(k == 0), stop=(k == NKC - 1)))
                t = top(lambda tn, k=k: tn.matmul(pb[7][0:1, :], onesb[:, 0:1], t1[:],
                                                  start=(k == 0), stop=(k == NKC - 1)))
                p.wait("vector", "TE", t)
                p.wait("scalar", "TE", t)
            vred(lambda v, tq=tq: v.tensor_copy(
                out=rowsA[0:1, 3 + tq * TT:3 + (tq + 1) * TT], in_=pb[6][0:1, :]))
            vv = vred(lambda v, tq=tq: v.tensor_copy(
                out=rowsB[0:1, 3 + tq * TT:3 + (tq + 1) * TT], in_=pb[7][0:1, :]))
            p.wait("tensor", "VE", vv)
        if with_halo_fn is not None:
            for k in range(NKC):
                with_halo_fn(k)
                v = p.cur("VE")
                p.wait("scalar", "VE", v)
                s = sop(lambda s: s.square(out=t1[:, 0:3], in_=t0[:, 0:3]))
                p.wait("tensor", "SE", s)
                p.wait("tensor", "VE", v)
                top(lambda tn, k=k: tn.matmul(pb[6][0:1, 0:3], onesb[:, 0:1], t0[:, 0:3],
                                              start=(k == 0), stop=(k == NKC - 1)))
                t = top(lambda tn, k=k: tn.matmul(pb[7][0:1, 0:3], onesb[:, 0:1], t1[:, 0:3],
                                                  start=(k == 0), stop=(k == NKC - 1)))
                p.wait("vector", "TE", t)
                p.wait("scalar", "TE", t)
            vred(lambda v: v.tensor_copy(out=rowsA[0:1, 0:3], in_=pb[6][0:1, 0:3]))
            vv = vred(lambda v: v.tensor_copy(out=rowsB[0:1, 0:3], in_=pb[7][0:1, 0:3]))
            p.wait("tensor", "VE", vv)
        w = BCW if with_halo_fn is not None else n + 3
        lo = 0 if with_halo_fn is not None else 3
        m = w - lo
        vred(lambda v, lo=lo, m=m: v.tensor_scalar_mul(
            rowsA[0:1, lo:lo + m], rowsA[0:1, lo:lo + m], 1.0 / D))
        vred(lambda v, lo=lo, m=m: v.tensor_scalar_mul(
            rowsB[0:1, lo:lo + m], rowsB[0:1, lo:lo + m], 1.0 / D))
        vred(lambda v, lo=lo, m=m: v.tensor_tensor(
            qtmp[0:1, 0:m], rowsA[0:1, lo:lo + m], rowsA[0:1, lo:lo + m], ALU.mult))
        vred(lambda v, lo=lo, m=m: v.tensor_sub(
            rowsB[0:1, lo:lo + m], rowsB[0:1, lo:lo + m], qtmp[0:1, 0:m]))
        vv = vred(lambda v, lo=lo, m=m: v.tensor_scalar_add(
            rowsB[0:1, lo:lo + m], rowsB[0:1, lo:lo + m], 1e-5))
        p.wait("scalar", "VE", vv)
        s = sop(lambda s, lo=lo, m=m: s.activation(
            out=rowsB[0:1, lo:lo + m], in_=rowsB[0:1, lo:lo + m], func=AF.Sqrt))
        p.wait("vector", "SE", s)
        vv = vred(lambda v, lo=lo, m=m: v.reciprocal(
            rowsB[0:1, lo:lo + m], rowsB[0:1, lo:lo + m]))
        p.wait("tensor", "VE", vv)
        p.wait("scalar", "VE", vv)
        bcast_row(bc[:, slot_mu * BCW + lo:slot_mu * BCW + w], rowsA[0:1, lo:w], m)
        bcast_row(bc[:, slot_rs * BCW + lo:slot_rs * BCW + w], rowsB[0:1, lo:w], m)
        return p.cur("VE")

    if limit < 2: return _finish(nc, p, e, ctx, outp, dbg, dbgs, t0, outp_dummy=True)
    xa = arena
    for k in range(NKC):
        ld(xa[:, k * TPC:(k + 1) * TPC], xT[k * 128:(k + 1) * 128, :])
    lh = ld(qtmp[:, 0:24].rearrange("p (k f) -> k p f", f=3) if False else
            qtmp[:, 0:24], None) if False else None
    ld(qbuf[:, 0:128], xh)
    lx = p.cur("LD")
    p.wait("scalar", "LD", lx)
    p.wait("vector", "LD", lx)

    def s1_src(k, tq):
        vop(lambda v, k=k, tq=tq: v.tensor_copy(
            out=t0[:], in_=xa[:, k * TPC + tq * TT:k * TPC + (tq + 1) * TT]))

    def s1_halo(k):
        vop(lambda v, k=k: v.tensor_copy(out=t0[:, 0:3], in_=qbuf[:, k * 3:(k + 1) * 3]))

    stats_pass(s1_src, 0, 1)
    vred(lambda v: v.tensor_copy(out=dbgs[0:1, 26:29], in_=rowsA[0:1, 3:6]))
    vred(lambda v: v.tensor_copy(out=dbgs[0:1, 29:32], in_=rowsB[0:1, 3:6]))
    join("vector", "scalar", "gpsimd")

    if limit < 3: return _finish(nc, p, e, ctx, outp, dbg, dbgs, t0, outp_dummy=True)
    def xn_make(k, tq):
        lo = tq * TT
        vop(lambda v, k=k, lo=lo: v.tensor_sub(
            t0[:], xa[:, k * TPC + lo:k * TPC + lo + TT], bcs(0, 3 + lo, TT)))
        return vop(lambda v, lo=lo: v.tensor_mul(t0[:], t0[:], bcs(1, 3 + lo, TT)))

    def s2_src(k, tq):
        v = xn_make(k, tq)
        p.wait("gpsimd", "VE", v)
        g = gst(xn_sp[k * 128:(k + 1) * 128, 3 + tq * TT:3 + (tq + 1) * TT], t0[:])
        vop(lambda v, k=k, tq=tq: v.tensor_copy(
            out=pil[:, (tq * NKC + k) * 8:(tq * NKC + k) * 8 + 8], in_=t0[:, 0:TT:64]))
        p.wait("vector", "GD", g)

    def s2_halo(k):
        vop(lambda v, k=k: v.tensor_sub(t0[:, 0:3], qbuf[:, k * 3:(k + 1) * 3],
                                        bcs(0, 0, 3)))
        v = vop(lambda v: v.tensor_mul(t0[:, 0:3], t0[:, 0:3], bcs(1, 0, 3)))
        p.wait("gpsimd", "VE", v)
        g = gst(xn_sp[k * 128:(k + 1) * 128, 0:3], t0[:, 0:3])
        p.wait("vector", "GD", g)

    stats_pass(s2_src, 2, 3)
    vred(lambda v: v.tensor_copy(out=dbgs[0:1, 32:35], in_=rowsA[0:1, 3:6]))
    vred(lambda v: v.tensor_copy(out=dbgs[0:1, 35:38], in_=rowsB[0:1, 3:6]))
    vred(lambda v: v.tensor_copy(out=dbgs[0:1, 38:40], in_=bc[0:1, 2 * BCW + 3:2 * BCW + 5]))
    join("vector", "scalar", "gpsimd")

    def h1f_tile(k, tq, dst):
        lo = tq * TT
        p.wait("gpsimd", "VE", p.cur("VE"))
        g = gld(t3[:, 0:TT], xn_sp[k * 128:(k + 1) * 128, 3 + lo:3 + lo + TT])
        p.wait("vector", "GD", g)
        vop(lambda v, lo=lo, d=dst: v.tensor_sub(d, t3[:, 0:TT], bcs(2, 3 + lo, TT)))
        vop(lambda v, lo=lo, d=dst: v.tensor_mul(d, d, bcs(3, 3 + lo, TT)))
        return vop(lambda v, k=k, d=dst: v.tensor_scalar(
            d, d, ccol("ffn_g", k), ccol("ffn_b", k), ALU.mult, ALU.add))

    def qpath(pilw, plo, phi, Mv, datagen, ntiles, sbase, scol):
        W = ntiles * NTOP
        vop(lambda v: v.memset(qacc[:, 0:NPIL], 0.0))
        for j in range(NPIL):
            th = plo + (phi - plo) * (j + 1) / (NPIL + 1)
            vop(lambda v, th=th, j=j, w=pilw: v.tensor_scalar(
                qtmp[:, 0:w], pil[:, 0:w], float(th), qacc[:, j:j + 1],
                ALU.is_gt, ALU.add, accum_out=qacc[:, j:j + 1]))
        pass
        pfold(qrow[0:1, 0:NPIL], qacc[:, 0:NPIL], NPIL, ALU.add)
        allreduce(NPIL, ALU.add, qrow[0:1, 0:NPIL], qrow[0:1, 0:NPIL])
        vred(lambda v: v.tensor_copy(out=dbgs[0:1, 60:64], in_=qrow[0:1, 12:16]))
        ptail = (Mv / 64.0) * 0.005
        for j in range(NPIL):
            th = plo + (phi - plo) * (j + 1) / (NPIL + 1)
            vred(lambda v, th=th, j=j: v.memset(qtmp[0:1, 64 + j:65 + j], float(th)))
        vred(lambda v: v.tensor_single_scalar(qtmp[0:1, 0:NPIL], qrow[0:1, 0:NPIL],
                                             float(ptail), ALU.is_gt))
        vred(lambda v: v.tensor_mul(qtmp[0:1, 0:NPIL], qtmp[0:1, 0:NPIL],
                                   qtmp[0:1, 64:64 + NPIL]))
        vred(lambda v: v.tensor_reduce(qrow[0:1, 60:61], qtmp[0:1, 0:NPIL], AXX, ALU.max))
        vred(lambda v: v.tensor_scalar_max(qrow[0:1, 60:61], qrow[0:1, 60:61], float(plo)))
        vred(lambda v: v.tensor_scalar_add(qrow[0:1, 61:62], qrow[0:1, 60:61], -DENSE_W))
        vred(lambda v: v.tensor_scalar_add(qrow[0:1, 62:63], qrow[0:1, 60:61], DENSE_W))
        vred(lambda v: v.tensor_scalar_add(qrow[0:1, 63:64], qrow[0:1, 60:61], EXT_W))
        vred(lambda v: v.tensor_copy(out=dbgs[0:1, 56:60], in_=qrow[0:1, 60:64]))
        step = 2.0 * DENSE_W / (NDEN + 1)
        for j in range(NDEN):
            vred(lambda v, j=j, st=step: v.tensor_scalar_add(
                qtmp[0:1, 64 + j:65 + j], qrow[0:1, 61:62], float(st * (j + 1))))
        bcast_row(qcol[:, 0:NDEN], qtmp[0:1, 64:64 + NDEN], NDEN)
        bcast_row(qcol[:, 30:31], qrow[0:1, 63:64], 1)
        vop(lambda v: v.memset(qacc[:, 0:NDEN + 1], 0.0))
        vop(lambda v, W=W: v.memset(qbuf[:, 0:W], -1.0))
        for i in range(ntiles):
            datagen(i, t2[:, 0:TT])
            for j in range(NDEN):
                vop(lambda v, j=j: v.tensor_scalar(
                    qtmp[:, 0:TT], t2[:], qcol[:, j:j + 1], 0.0,
                    ALU.is_gt, ALU.add, accum_out=qacc[:, 32 + j:33 + j]))
            n = vop(lambda v: v.tensor_scalar(
                qtmp[:, 0:TT], t2[:], qcol[:, 30:31], 0.0,
                ALU.is_gt, ALU.add, accum_out=qacc[:, 32 + NDEN:33 + NDEN]))
            p.wait("vector", "VE", n)
            for j in range(NDEN + 1):
                vop(lambda v, j=j: v.tensor_add(qacc[:, j:j + 1], qacc[:, j:j + 1],
                                                qacc[:, 32 + j:33 + j]))
            vop(lambda v: v.tensor_scalar(qtmp[:, 0:TT], t2[:], qcol[:, 30:31], 0.0,
                                          ALU.is_le, ALU.add))
            vop(lambda v: v.tensor_mul(qtmp[:, TT:2 * TT], t2[:], qtmp[:, 0:TT]))
            vop(lambda v: v.tensor_scalar_add(qtmp[:, 0:TT], qtmp[:, 0:TT], -1.0))
            vop(lambda v: v.tensor_add(qtmp[:, TT:2 * TT], qtmp[:, TT:2 * TT],
                                       qtmp[:, 0:TT]))
            for r in range(NTOP // 8):
                n = vop(lambda v, i=i, r=r: v.max(
                    out=qbuf[:, i * NTOP + r * 8:i * NTOP + r * 8 + 8],
                    in_=qtmp[:, TT:2 * TT]))
                p.wait("vector", "VE", n)
                n = vop(lambda v, i=i, r=r: v.match_replace(
                    out=qtmp[:, TT:2 * TT],
                    in_to_replace=qbuf[:, i * NTOP + r * 8:i * NTOP + r * 8 + 8],
                    in_values=qtmp[:, TT:2 * TT], imm_value=-1.0))
                p.wait("vector", "VE", n)
        pfold(qrow[0:1, 0:NDEN + 1], qacc[:, 0:NDEN + 1], NDEN + 1, ALU.add)
        allreduce(NDEN + 1, ALU.add, qrow[0:1, 0:NDEN + 1], qrow[0:1, 0:NDEN + 1])
        vred(lambda v: v.memset(qrow[0:1, 100:101], float(Mv)))
        vred(lambda v: v.tensor_scalar_add(qrow[0:1, 100:101], qrow[0:1, 100:101], -1.0))
        vred(lambda v: v.tensor_scalar_mul(qrow[0:1, 101:102], qrow[0:1, 100:101], C995_HI))
        vred(lambda v: v.tensor_scalar_mul(qrow[0:1, 102:103], qrow[0:1, 100:101], C995_LO))
        vred(lambda v: v.tensor_add(qrow[0:1, 101:102], qrow[0:1, 101:102],
                                   qrow[0:1, 102:103]))
        vred(lambda v: v.tensor_copy(out=msk[0:1, 0:1], in_=qrow[0:1, 101:102]))
        vred(lambda v: v.tensor_copy(out=qrow[0:1, 103:104], in_=msk[0:1, 0:1]))
        vred(lambda v: v.tensor_tensor(qrow[0:1, 104:105], qrow[0:1, 103:104],
                                      qrow[0:1, 101:102], ALU.is_gt))
        vred(lambda v: v.tensor_sub(qrow[0:1, 103:104], qrow[0:1, 103:104],
                                   qrow[0:1, 104:105]))
        vred(lambda v: v.tensor_sub(qrow[0:1, 105:106], qrow[0:1, 101:102],
                                   qrow[0:1, 103:104]))
        vred(lambda v: v.memset(qrow[0:1, 107:108], 1.0))
        vred(lambda v: v.tensor_sub(qrow[0:1, 106:107], qrow[0:1, 107:108],
                                   qrow[0:1, 105:106]))
        vred(lambda v: v.tensor_single_scalar(qrow[0:1, 108:109], qrow[0:1, 105:106],
                                             0.0, ALU.is_gt))
        vred(lambda v: v.tensor_add(qrow[0:1, 108:109], qrow[0:1, 103:104],
                                   qrow[0:1, 108:109]))
        vred(lambda v: v.tensor_sub(qrow[0:1, 110:111], qrow[0:1, 100:101],
                                   qrow[0:1, 103:104]))
        vred(lambda v: v.tensor_sub(qrow[0:1, 111:112], qrow[0:1, 100:101],
                                   qrow[0:1, 108:109]))
        for j in range(NDEN):
            vred(lambda v, j=j, st=step: v.tensor_scalar_add(
                qtmp[0:1, 64 + j:65 + j], qrow[0:1, 61:62], float(st * (j + 1))))
        vred(lambda v: v.tensor_scalar(qtmp[0:1, 0:NDEN], qrow[0:1, 0:NDEN],
                                      qrow[0:1, 110:111], 0.0, ALU.is_gt, ALU.add))
        vred(lambda v: v.tensor_mul(qtmp[0:1, 32:32 + NDEN], qtmp[0:1, 0:NDEN],
                                   qtmp[0:1, 64:64 + NDEN]))
        vred(lambda v: v.tensor_reduce(qrow[0:1, 116:117], qtmp[0:1, 32:32 + NDEN],
                                       AXX, ALU.max))
        vred(lambda v: v.tensor_tensor(qrow[0:1, 112:113], qrow[0:1, 116:117],
                                      qrow[0:1, 61:62], ALU.max))
        vred(lambda v: v.tensor_scalar(qtmp[0:1, 0:NDEN], qrow[0:1, 0:NDEN],
                                      qrow[0:1, 111:112], 0.0, ALU.is_le, ALU.add))
        vred(lambda v: v.tensor_scalar(qtmp[0:1, 32:32 + NDEN], qtmp[0:1, 0:NDEN],
                                      -1.0, -1e6, ALU.add, ALU.mult))
        vred(lambda v: v.tensor_add(qtmp[0:1, 32:32 + NDEN], qtmp[0:1, 32:32 + NDEN],
                                   qtmp[0:1, 64:64 + NDEN]))
        vred(lambda v: v.tensor_reduce(qrow[0:1, 116:117], qtmp[0:1, 32:32 + NDEN],
                                       AXX, ALU.min))
        vred(lambda v: v.tensor_tensor(qrow[0:1, 113:114], qrow[0:1, 116:117],
                                      qrow[0:1, 63:64], ALU.min))
        vred(lambda v: v.tensor_copy(out=qrow[0:1, 114:115], in_=qrow[0:1, NDEN:NDEN + 1]))
        for rk in range(2):
            vred(lambda v, rk=rk: v.tensor_copy(
                out=qrow[0:1, 120 + 4 * rk:121 + 4 * rk], in_=qrow[0:1, 112:113]))
            vred(lambda v, rk=rk: v.tensor_copy(
                out=qrow[0:1, 121 + 4 * rk:122 + 4 * rk], in_=qrow[0:1, 113:114]))
        for rnd in range(NBIS):
            for rk in range(2):
                vred(lambda v, rk=rk: v.tensor_sub(
                    qrow[0:1, 122 + 4 * rk:123 + 4 * rk],
                    qrow[0:1, 121 + 4 * rk:122 + 4 * rk],
                    qrow[0:1, 120 + 4 * rk:121 + 4 * rk]))
                for j in range(TBIS):
                    fr = (j + 1) / (TBIS + 1.0)
                    vred(lambda v, rk=rk, j=j, fr=fr: v.tensor_scalar_mul(
                        qtmp[0:1, 64 + rk * TBIS + j:65 + rk * TBIS + j],
                        qrow[0:1, 122 + 4 * rk:123 + 4 * rk], float(fr)))
                    vred(lambda v, rk=rk, j=j: v.tensor_add(
                        qtmp[0:1, 64 + rk * TBIS + j:65 + rk * TBIS + j],
                        qtmp[0:1, 64 + rk * TBIS + j:65 + rk * TBIS + j],
                        qrow[0:1, 120 + 4 * rk:121 + 4 * rk]))
            bcast_row(qcol[:, 0:2 * TBIS], qtmp[0:1, 64:64 + 2 * TBIS], 2 * TBIS)
            vop(lambda v: v.memset(qacc[:, 0:2 * TBIS], 0.0))
            for j in range(2 * TBIS):
                vop(lambda v, j=j, W=W: v.tensor_scalar(
                    qtmp[:, 1024:1024 + W], qbuf[:, 0:W], qcol[:, j:j + 1],
                    qacc[:, j:j + 1], ALU.is_gt, ALU.add,
                    accum_out=qacc[:, j:j + 1]))
            pfold(qrow[0:1, 0:2 * TBIS], qacc[:, 0:2 * TBIS], 2 * TBIS, ALU.add)
            if rnd == 0:
                vred(lambda v: v.tensor_copy(out=dbgs[0:1, 56:58], in_=qrow[0:1, 0:2]))
                vred(lambda v: v.tensor_copy(out=dbgs[0:1, 58:60], in_=qtmp[0:1, 64:66]))
            allreduce(2 * TBIS, ALU.add, qrow[0:1, 0:2 * TBIS], qrow[0:1, 0:2 * TBIS])
            if rnd == 0:
                vred(lambda v: v.tensor_copy(out=dbgs[0:1, 60:62], in_=qrow[0:1, 0:2]))
            vred(lambda v: v.tensor_scalar(qrow[0:1, 0:2 * TBIS], qrow[0:1, 0:2 * TBIS],
                                          qrow[0:1, 114:115], 0.0, ALU.add, ALU.add))
            for rk in range(2):
                rr = 110 + rk
                vred(lambda v, rk=rk, rr=rr: v.tensor_scalar(
                    qtmp[0:1, 0:TBIS], qrow[0:1, rk * TBIS:(rk + 1) * TBIS],
                    qrow[0:1, rr:rr + 1], 0.0, ALU.is_gt, ALU.add))
                vred(lambda v, rk=rk: v.tensor_mul(
                    qtmp[0:1, 32:32 + TBIS], qtmp[0:1, 0:TBIS],
                    qtmp[0:1, 64 + rk * TBIS:64 + (rk + 1) * TBIS]))
                vred(lambda v: v.tensor_reduce(qrow[0:1, 116:117],
                                               qtmp[0:1, 32:32 + TBIS], AXX, ALU.max))
                vred(lambda v, rk=rk: v.tensor_tensor(
                    qrow[0:1, 120 + 4 * rk:121 + 4 * rk],
                    qrow[0:1, 120 + 4 * rk:121 + 4 * rk], qrow[0:1, 116:117], ALU.max))
                vred(lambda v, rk=rk, rr=rr: v.tensor_scalar(
                    qtmp[0:1, 0:TBIS], qrow[0:1, rk * TBIS:(rk + 1) * TBIS],
                    qrow[0:1, rr:rr + 1], 0.0, ALU.is_le, ALU.add))
                vred(lambda v, rk=rk: v.tensor_mul(
                    qtmp[0:1, 32:32 + TBIS], qtmp[0:1, 0:TBIS],
                    qtmp[0:1, 64 + rk * TBIS:64 + (rk + 1) * TBIS]))
                vred(lambda v, rk=rk: v.tensor_scalar(
                    qtmp[0:1, 0:TBIS], qtmp[0:1, 0:TBIS], -1.0, -1e6,
                    ALU.add, ALU.mult))
                vred(lambda v, rk=rk: v.tensor_add(
                    qtmp[0:1, 32:32 + TBIS], qtmp[0:1, 32:32 + TBIS], qtmp[0:1, 0:TBIS]))
                vred(lambda v: v.tensor_reduce(qrow[0:1, 116:117],
                                               qtmp[0:1, 32:32 + TBIS], AXX, ALU.min))
                vred(lambda v, rk=rk: v.tensor_tensor(
                    qrow[0:1, 121 + 4 * rk:122 + 4 * rk],
                    qrow[0:1, 121 + 4 * rk:122 + 4 * rk], qrow[0:1, 116:117], ALU.min))
        for rk in range(2):
            bcast_row(qcol[:, 40:41], qrow[0:1, 120 + 4 * rk:121 + 4 * rk], 1)
            bcast_row(qcol[:, 41:42], qrow[0:1, 121 + 4 * rk:122 + 4 * rk], 1)
            h = W // 2
            for part in range(2):
                lo_, hi_ = part * h, (part + 1) * h
                vop(lambda v, lo_=lo_, hi_=hi_: v.tensor_scalar(
                    qtmp[:, 0:h] if False else qtmp[:, lo_:hi_], qbuf[:, lo_:hi_],
                    qcol[:, 40:41], 0.0, ALU.is_gt, ALU.add))
                vop(lambda v, lo_=lo_, hi_=hi_: v.tensor_mul(
                    qtmp[:, lo_:hi_], qtmp[:, lo_:hi_], qbuf[:, lo_:hi_]))
                vop(lambda v, lo_=lo_, hi_=hi_: v.tensor_scalar(
                    qtmp[:, 2048 + lo_:2048 + hi_], qbuf[:, lo_:hi_],
                    qcol[:, 41:42], 0.0, ALU.is_le, ALU.add))
                vop(lambda v, lo_=lo_, hi_=hi_: v.tensor_mul(
                    qtmp[:, lo_:hi_], qtmp[:, lo_:hi_],
                    qtmp[:, 2048 + lo_:2048 + hi_]))
            vred(lambda v, W=W: v.tensor_reduce(qcol[:, 42:43], qtmp[:, 0:W],
                                                AXX, ALU.max))
            pfold(qrow[0:1, 117 + rk:118 + rk], qcol[:, 42:43], 1, ALU.max)
        allreduce(2, ALU.max, qrow[0:1, 117:119], qrow[0:1, 117:119])
        vred(lambda v: v.tensor_copy(out=dbgs[0:1, 40:41], in_=qrow[0:1, 60:61]))
        vred(lambda v: v.tensor_copy(out=dbgs[0:1, 41:42], in_=qrow[0:1, 61:62]))
        vred(lambda v: v.tensor_copy(out=dbgs[0:1, 42:43], in_=qrow[0:1, 63:64]))
        vred(lambda v: v.tensor_copy(out=dbgs[0:1, 43:44], in_=qrow[0:1, 101:102]))
        vred(lambda v: v.tensor_copy(out=dbgs[0:1, 44:45], in_=qrow[0:1, 103:104]))
        vred(lambda v: v.tensor_copy(out=dbgs[0:1, 45:46], in_=qrow[0:1, 110:111]))
        vred(lambda v: v.tensor_copy(out=dbgs[0:1, 46:47], in_=qrow[0:1, NDEN:NDEN + 1]))
        vred(lambda v: v.tensor_copy(out=dbgs[0:1, 47:48], in_=qrow[0:1, 112:113]))
        vred(lambda v: v.tensor_copy(out=dbgs[0:1, 48:49], in_=qrow[0:1, 113:114]))
        vred(lambda v: v.tensor_copy(out=dbgs[0:1, 49:50], in_=qrow[0:1, 120:121]))
        vred(lambda v: v.tensor_copy(out=dbgs[0:1, 50:51], in_=qrow[0:1, 121:122]))
        vred(lambda v: v.tensor_copy(out=dbgs[0:1, 51:52], in_=qrow[0:1, 117:118]))
        vred(lambda v: v.tensor_copy(out=dbgs[0:1, 52:53], in_=qrow[0:1, 118:119]))
        vred(lambda v: v.tensor_copy(out=dbgs[0:1, 53:54], in_=qrow[0:1, 0:1]))
        vred(lambda v: v.tensor_copy(out=dbgs[0:1, 54:55], in_=qrow[0:1, 105:106]))
        vred(lambda v: v.tensor_copy(out=dbgs[0:1, 55:56], in_=qrow[0:1, 106:107]))
        vred(lambda v: v.tensor_mul(qrow[0:1, 117:118], qrow[0:1, 117:118],
                                   qrow[0:1, 106:107]))
        vred(lambda v: v.tensor_mul(qrow[0:1, 118:119], qrow[0:1, 118:119],
                                   qrow[0:1, 105:106]))
        vred(lambda v, sb=sbase: v.tensor_add(qrow[0:1, sb:sb + 1], qrow[0:1, 117:118],
                                             qrow[0:1, 118:119]))
        vred(lambda v, sb=sbase: v.tensor_scalar_max(qrow[0:1, sb:sb + 1],
                                                    qrow[0:1, sb:sb + 1], 1e-5))
        vred(lambda v, sb=sbase: v.reciprocal(qrow[0:1, sb + 1:sb + 2],
                                              qrow[0:1, sb:sb + 1]))
        vred(lambda v, sb=sbase: v.tensor_scalar_mul(qrow[0:1, sb + 1:sb + 2],
                                                    qrow[0:1, sb + 1:sb + 2], 127.0))
        vred(lambda v, sb=sbase: v.reciprocal(qrow[0:1, sb + 2:sb + 3],
                                              qrow[0:1, sb + 1:sb + 2]))
        bcast_row(qcol[:, scol:scol + 2], qrow[0:1, sbase + 1:sbase + 3], 2)

    if limit < 4: return _finish(nc, p, e, ctx, outp, dbg, dbgs, t0, outp_dummy=True)
    for tq in range(NTQ):
        for k in range(NKC):
            base = (tq * NKC + k) * 8
            vop(lambda v, tq=tq, base=base: v.tensor_copy(
                out=qtmp[:, base:base + 8],
                in_=bc[:, 2 * BCW + 3 + tq * TT:2 * BCW + 3 + (tq + 1) * TT:64]))
            vop(lambda v, tq=tq, base=base: v.tensor_copy(
                out=qtmp[:, 1024 + base:1024 + base + 8],
                in_=bc[:, 3 * BCW + 3 + tq * TT:3 * BCW + 3 + (tq + 1) * TT:64]))
            vop(lambda v, base=base: v.tensor_sub(
                pil[:, base:base + 8], pil[:, base:base + 8], qtmp[:, base:base + 8]))
            vop(lambda v, base=base: v.tensor_mul(
                pil[:, base:base + 8], pil[:, base:base + 8],
                qtmp[:, 1024 + base:1024 + base + 8]))
            vop(lambda v, k=k, base=base: v.tensor_scalar(
                pil[:, base:base + 8], pil[:, base:base + 8],
                ccol("ffn_g", k), ccol("ffn_b", k), ALU.mult, ALU.add))
    v = p.cur("VE")
    p.wait("scalar", "VE", v)
    s = sop(lambda s: s.activation(out=pil[:, 0:256], in_=pil[:, 0:256], func=AF.Abs))
    p.wait("vector", "SE", s)

    def q1_datagen(i, dst):
        k, tq = i % NKC, i // NKC
        v = h1f_tile(k, tq, dst)
        p.wait("scalar", "VE", v)
        s = sop(lambda s, d=dst: s.activation(out=d, in_=d, func=AF.Abs))
        p.wait("vector", "SE", s)

    qpath(256, Q1_PLO, Q1_PHI, M1, q1_datagen, NKC * NTQ, sbase=150, scol=44)
    vred(lambda v: v.tensor_copy(out=dbgs[0:1, 0:3], in_=qrow[0:1, 150:153]))
    join("vector", "scalar", "gpsimd", "tensor")

    if limit < 5: return _finish(nc, p, e, ctx, outp, dbg, dbgs, t0, outp_dummy=True)
    hc = arena
    for tq in range(NTQ):
        for k in range(NKC):
            lo = tq * TT
            p.wait("gpsimd", "VE", p.cur("VE"))
            g = gld(t3[:, 0:TT + 3], xn_sp[k * 128:(k + 1) * 128, lo:lo + TT + 3])
            p.wait("vector", "GD", g)
            vop(lambda v, lo=lo: v.tensor_sub(t3[:, 0:TT + 3], t3[:, 0:TT + 3],
                                              bcs(2, lo, TT + 3)))
            vop(lambda v, lo=lo: v.tensor_mul(t3[:, 0:TT + 3], t3[:, 0:TT + 3],
                                              bcs(3, lo, TT + 3)))
            vop(lambda v, k=k: v.tensor_scalar(
                t3[:, 0:TT + 3], t3[:, 0:TT + 3],
                ccol("ln1_g", k), ccol("ln1_b", k), ALU.mult, ALU.add))
            if tq == 0:
                vop(lambda v: v.tensor_single_scalar(out=t3[:, 0:3], in_=t3[:, 0:3], scalar=pcb[:, 0:1], op=ALU.mult))
            dst = hc[:, k * TPC + tq * TT:k * TPC + (tq + 1) * TT]
            vop(lambda v, k=k, d=dst: v.tensor_scalar(
                d, t3[:, 0:TT], dwb[:, k * 4:k * 4 + 1], 0.0, ALU.mult, ALU.add))
            for tap in range(1, 4):
                vop(lambda v, k=k, tap=tap, d=dst: v.scalar_tensor_tensor(
                    d, t3[:, tap:tap + TT], dwb[:, k * 4 + tap:k * 4 + tap + 1], d,
                    ALU.mult, ALU.add))
            vop(lambda v, k=k, d=dst: v.tensor_scalar(
                d, d, ccol("dw_b", k), 0.0, ALU.add, ALU.add))
    p.wait("tensor", "VE", p.cur("VE"))
    p.wait("scalar", "VE", p.cur("VE"))
    g = gld(qtmp[:, 0:128], xh)
    p.wait("vector", "GD", g)
    p.wait("scalar", "GD", g)
    vred(lambda v: v.memset(t0[:], 0.0))
    for k in range(NKC):
        vred(lambda v, k=k: v.tensor_copy(out=t0[:, 0:3], in_=qtmp[:, k * 3:k * 3 + 3]))
        if k == 0:
            vred(lambda v: v.tensor_copy(out=dbgs[0:1, 36:39], in_=t0[0:1, 0:3]))
        vred(lambda v, k=k: v.tensor_copy(out=h1h[:, k * 3:k * 3 + 3], in_=t0[:, 0:3]))
        s_ = sop(lambda sc: sc.square(out=t1[:], in_=t0[:]))
        p.wait("tensor", "SE", s_)
        top(lambda tn, k=k: tn.matmul(pb[6][0:1, :], onesb[:, 0:1], t0[:],
                                      start=(k == 0), stop=(k == NKC - 1)))
        t = top(lambda tn, k=k: tn.matmul(pb[7][0:1, :], onesb[:, 0:1], t1[:],
                                          start=(k == 0), stop=(k == NKC - 1)))
        p.wait("vector", "TE", t)
        p.wait("scalar", "TE", t)
    def _halo_rowstats(dst_mu, dst_rs):
        vred(lambda v: v.tensor_copy(out=qrow[0:1, 200:203], in_=pb[6][0:1, 0:3]))
        vred(lambda v: v.tensor_copy(out=qrow[0:1, 204:207], in_=pb[7][0:1, 0:3]))
        vred(lambda v: v.tensor_scalar_mul(qrow[0:1, 200:203], qrow[0:1, 200:203], 1.0 / D))
        vred(lambda v: v.tensor_scalar_mul(qrow[0:1, 204:207], qrow[0:1, 204:207], 1.0 / D))
        vred(lambda v: v.tensor_tensor(qrow[0:1, 208:211], qrow[0:1, 200:203],
                                       qrow[0:1, 200:203], ALU.mult))
        vred(lambda v: v.tensor_sub(qrow[0:1, 204:207], qrow[0:1, 204:207],
                                    qrow[0:1, 208:211]))
        vv = vred(lambda v: v.tensor_scalar_add(qrow[0:1, 204:207], qrow[0:1, 204:207], 1e-5))
        p.wait("scalar", "VE", vv)
        s_ = sop(lambda sc: sc.activation(out=qrow[0:1, 204:207], in_=qrow[0:1, 204:207],
                                          func=AF.Sqrt))
        p.wait("vector", "SE", s_)
        vred(lambda v: v.reciprocal(qrow[0:1, 204:207], qrow[0:1, 204:207]))
        bcast_row(dst_mu, qrow[0:1, 200:203], 3)
        bcast_row(dst_rs, qrow[0:1, 204:207], 3)
    _halo_rowstats(qcol[:, 56:59], qcol[:, 59:62])
    vred(lambda v: v.tensor_copy(out=dbgs[0:1, 62:63], in_=qcol[0:1, 56:57]))
    vred(lambda v: v.tensor_copy(out=dbgs[0:1, 63:64], in_=qcol[0:1, 59:60]))
    p.wait("tensor", "VE", p.cur("VE"))
    vred(lambda v: v.memset(t0[:], 0.0))
    for k in range(NKC):
        vop(lambda v, k=k: v.tensor_sub(t0[:, 0:3], h1h[:, k * 3:k * 3 + 3], qcol[:, 56:59]))
        v = vop(lambda v: v.tensor_mul(t0[:, 0:3], t0[:, 0:3], qcol[:, 59:62]))
        vred(lambda v, k=k: v.tensor_copy(out=h1h[:, k * 3:k * 3 + 3], in_=t0[:, 0:3]))
        p.wait("scalar", "VE", p.cur("VE"))
        s_ = sop(lambda sc: sc.square(out=t1[:], in_=t0[:]))
        p.wait("tensor", "SE", s_)
        top(lambda tn, k=k: tn.matmul(pb[6][0:1, :], onesb[:, 0:1], t0[:],
                                      start=(k == 0), stop=(k == NKC - 1)))
        t = top(lambda tn, k=k: tn.matmul(pb[7][0:1, :], onesb[:, 0:1], t1[:],
                                          start=(k == 0), stop=(k == NKC - 1)))
        p.wait("vector", "TE", t)
        p.wait("scalar", "TE", t)
    _halo_rowstats(qcol[:, 56:59], qcol[:, 59:62])
    p.wait("vector", "TE", p.cur("TE"))
    for k in range(NKC):
        vop(lambda v, k=k: v.tensor_sub(h1h[:, k * 3:k * 3 + 3], h1h[:, k * 3:k * 3 + 3],
                                        qcol[:, 56:59]))
        vop(lambda v, k=k: v.tensor_mul(h1h[:, k * 3:k * 3 + 3], h1h[:, k * 3:k * 3 + 3],
                                        qcol[:, 59:62]))
        vop(lambda v, k=k: v.tensor_scalar(h1h[:, k * 3:k * 3 + 3], h1h[:, k * 3:k * 3 + 3],
                                           ccol("ln1_g", k), ccol("ln1_b", k),
                                           ALU.mult, ALU.add))
        vred(lambda v, k=k: v.tensor_single_scalar(
            out=h1h[:, k * 3:k * 3 + 3], in_=h1h[:, k * 3:k * 3 + 3],
            scalar=pcb[:, 0:1], op=ALU.mult))
    join("vector", "tensor", "sync", "scalar", "gpsimd")

    te_slab = [0, 0]
    for o in range(NKC):
        half = o % 2
        if o >= 2:
            p.wait("sync", "TE", te_slab[half])
        l = ld_tiles(half * 2048, pwT[0:NKC, o], 8)
        p.wait("tensor", "LD", l)
        for tq in range(NTQ):
            pbk = pb[(o * NTQ + tq) % 4]
            for k in range(NKC):
                top(lambda tn, k=k, half=half, pbk=pbk, tq=tq: tn.matmul(
                    pbk[:], wsl[:, half * 2048 + k * 128:half * 2048 + (k + 1) * 128],
                    hc[:, k * TPC + tq * TT:k * TPC + (tq + 1) * TT],
                    start=(k == 0), stop=(k == NKC - 1)))
            t = p.cur("TE")
            p.wait("scalar", "TE", t)
            s = sop(lambda sc, o=o, pbk=pbk: sc.activation(
                out=t0[:], in_=pbk[:], func=AF.Identity, bias=ccol("pw_b", o)))
            p.wait("vector", "SE", s)
            p.wait("gpsimd", "SE", s)
            g = gld(t1[:], xn_sp[o * 128:(o + 1) * 128, 3 + tq * TT:3 + (tq + 1) * TT])
            p.wait("vector", "GD", g)
            v = vop(lambda v: v.tensor_add(t0[:], t0[:], t1[:]))
            if o == 0 and tq == 0:
                vred(lambda v: v.tensor_copy(out=dbgs[0:1, 52:54], in_=t0[0:1, 0:2]))
            p.wait("gpsimd", "VE", v)
            g = gst(x1_sp[o * 128:(o + 1) * 128, tq * TT:(tq + 1) * TT], t0[:])
            p.wait("scalar", "GD", g)
            p.wait("tensor", "SE", p.cur("SE"))
        te_slab[half] = p.cur("TE")
    join("vector", "scalar", "gpsimd", "tensor", "sync")

    def s3_src(k, tq):
        p.wait("gpsimd", "VE", p.cur("VE"))
        g = gld(t0[:], x1_sp[k * 128:(k + 1) * 128, tq * TT:(tq + 1) * TT])
        p.wait("vector", "GD", g)
        p.wait("scalar", "GD", g)
        vop(lambda v: v.engine_nop())

    stats_pass(s3_src, 0, 1)
    join("vector", "scalar", "gpsimd", "tensor")

    if limit < 6: return _finish(nc, p, e, ctx, outp, dbg, dbgs, t0, outp_dummy=True)
    xq = arena
    for tq in range(NTQ):
        for k in range(NKC):
            dst = xq[:, k * TPC + tq * TT:k * TPC + (tq + 1) * TT]
            h1f_tile(k, tq, t0[:, 0:TT])
            vop(lambda v, d=dst: v.tensor_single_scalar(out=d, in_=t0[:], scalar=qcol[:, 44:45], op=ALU.mult))
            vop(lambda v, d=dst: v.tensor_copy(out=msk[:], in_=d))
            vop(lambda v, d=dst: v.tensor_copy(out=d, in_=msk[:]))
            vop(lambda v, d=dst: v.tensor_scalar_max(d, d, -128.0))
            vop(lambda v, d=dst: v.tensor_scalar_min(d, d, 127.0))
            vop(lambda v, d=dst: v.tensor_single_scalar(out=d, in_=d, scalar=qcol[:, 45:46], op=ALU.mult))
    join("vector", "tensor", "gpsimd", "scalar", "sync")

    te_slab = [0, 0]
    for o in range(NHC):
        half = o % 2
        if o >= 2:
            p.wait("sync", "TE", te_slab[half])
        l = ld_tiles(half * 2048, f1T[0:NKC, o], 8)
        p.wait("gpsimd", "LD", l)
        d = wsl[:, half * 2048:half * 2048 + 8 * 128]
        gop(lambda g, d=d: g.tensor_single_scalar(out=d, in_=d, scalar=qcol[:, 52:53], op=ALU.mult))
        gop(lambda g, d=d: g.tensor_copy(out=wint[:, 0:1024], in_=d))
        gop(lambda g, d=d: g.tensor_copy(out=d, in_=wint[:, 0:1024]))
        gop(lambda g, d=d: g.tensor_scalar_max(d, d, -1.0))
        gop(lambda g, d=d: g.tensor_scalar_min(d, d, 1.0))
        gq = gop(lambda g, d=d: g.tensor_single_scalar(out=d, in_=d, scalar=qcol[:, 50:51], op=ALU.mult))
        p.wait("tensor", "GE", gq)
        for tq in range(NTQ):
            pbk = pb[(o * NTQ + tq) % 4]
            for k in range(NKC):
                top(lambda tn, k=k, half=half, pbk=pbk, tq=tq: tn.matmul(
                    pbk[:], wsl[:, half * 2048 + k * 128:half * 2048 + (k + 1) * 128],
                    xq[:, k * TPC + tq * TT:k * TPC + (tq + 1) * TT],
                    start=(k == 0), stop=(k == NKC - 1)))
            t = p.cur("TE")
            p.wait("scalar", "TE", t)
            p.wait("scalar", "GD", p.cur("GD"))
            s = sop(lambda sc, pbk=pbk: sc.activation(out=t0[:], in_=pbk[:], func=AF.Gelu))
            p.wait("gpsimd", "SE", s)
            p.wait("vector", "SE", s)
            gst(ge_sp[o, :, tq * TT:(tq + 1) * TT], t0[:])
            vop(lambda v, o=o, tq=tq: v.tensor_copy(
                out=pil[:, (o * NTQ + tq) * 8:(o * NTQ + tq) * 8 + 8],
                in_=t0[:, 0:TT:64]))
            p.wait("tensor", "SE", p.cur("SE"))
        te_slab[half] = p.cur("TE")
    join("vector", "scalar", "gpsimd", "tensor", "sync")
    s = sop(lambda s: s.activation(out=pil[:, 0:1024], in_=pil[:, 0:1024], func=AF.Abs))
    p.wait("vector", "SE", s)

    if limit < 7: return _finish(nc, p, e, ctx, outp, dbg, dbgs, t0, outp_dummy=True)
    ga = arena
    for tq in range(NTQ):
        for k in range(NKC):
            p.wait("gpsimd", "VE", p.cur("VE"))
            g = gld(t3[:, 0:TT], x1_sp[k * 128:(k + 1) * 128, tq * TT:(tq + 1) * TT])
            p.wait("vector", "GD", g)
            dst = ga[:, k * TPC + tq * TT:k * TPC + (tq + 1) * TT]
            vop(lambda v, tq=tq, d=dst: v.tensor_sub(d, t3[:, 0:TT],
                                                     bcs(0, 3 + tq * TT, TT)))
            vop(lambda v, tq=tq, d=dst: v.tensor_mul(d, d, bcs(1, 3 + tq * TT, TT)))
            vop(lambda v, k=k, d=dst: v.tensor_scalar(
                d, d, ccol("ln2_g", k), ccol("ln2_b", k), ALU.mult, ALU.add))
    vop(lambda v: v.memset(stc[:, 0:12], 0.0))
    vop(lambda v: v.memset(stc[:, 12:24], 1.0))
    join("vector", "tensor", "scalar", "gpsimd", "sync")

    te_slab = [0, 0]
    for i in range(NEC):
        half = i % 2
        if i >= 2:
            p.wait("sync", "TE", te_slab[half])
        l = ld_tiles(half * 2048, fwT[0:NKC, i], 8)
        l = ld_tiles(half * 2048 + 8 * 128, fwT[0:NKC, NEC + i], 8)
        p.wait("tensor", "LD", l)
        for tq in range(NTQ):
            pbA = pb[4 + half * 2]
            pbB = pb[5 + half * 2]
            p.wait("tensor", "VE", p.cur("VE"))
            p.wait("tensor", "SE", p.cur("SE"))
            for k in range(NKC):
                top(lambda tn, k=k, half=half, pbA=pbA, tq=tq: tn.matmul(
                    pbA[:], wsl[:, half * 2048 + k * 128:half * 2048 + (k + 1) * 128],
                    ga[:, k * TPC + tq * TT:k * TPC + (tq + 1) * TT],
                    start=(k == 0), stop=(k == NKC - 1)))
            for k in range(NKC):
                top(lambda tn, k=k, half=half, pbB=pbB, tq=tq: tn.matmul(
                    pbB[:], wsl[:, half * 2048 + (8 + k) * 128:half * 2048 + (9 + k) * 128],
                    ga[:, k * TPC + tq * TT:k * TPC + (tq + 1) * TT],
                    start=(k == 0), stop=(k == NKC - 1)))
            t = p.cur("TE")
            p.wait("scalar", "TE", t)
            ct = cvb[:, half * 2 * TT:half * 2 * TT + TT]
            vt = cvb[:, half * 2 * TT + TT:half * 2 * TT + 2 * TT]
            p.wait("scalar", "VE", p.cur("VE"))
            p.wait("scalar", "GD", p.cur("GD"))
            sop(lambda sc, i=i, pbA=pbA, ct=ct: sc.activation(
                out=ct, in_=pbA[:], func=AF.Sigmoid, scale=-1.0,
                bias=fbb[:, 24 + i:25 + i]))
            sop(lambda sc, i=i, pbA=pbA: sc.activation(
                out=t0[:], in_=pbA[:], func=AF.Sigmoid, bias=fbb[:, i:i + 1]))
            sop(lambda sc, i=i, pbB=pbB: sc.activation(
                out=t1[:], in_=pbB[:], func=AF.Identity, bias=fbb[:, 12 + i:13 + i]))
            s = sop(lambda sc, i=i, pbB=pbB: sc.activation(
                out=t2[:], in_=pbB[:], func=AF.Sigmoid, bias=fbb[:, 12 + i:13 + i]))
            p.wait("vector", "SE", s)
            vop(lambda v: v.tensor_single_scalar(msk[:], t1[:], 0.0, ALU.is_ge))
            if i == 0 and tq == 3:
                vred(lambda v: v.tensor_copy(out=dbgs[0:1, 30:32], in_=t1[0:1, 510:512]))
                vred(lambda v: v.tensor_copy(out=dbgs[0:1, 32:33], in_=msk[0:1, 511:512]))
            vop(lambda v: v.tensor_scalar_add(t1[:], t1[:], 0.5))
            vop(lambda v: v.select(out=t2[:], mask=msk[:], on_true=t1[:], on_false=t2[:]))
            v = vop(lambda v, vt=vt: v.tensor_mul(vt, t0[:], t2[:]))
            p.wait("gpsimd", "VE", v)
            gst(cv_sp[i, 0, :, tq * TT:(tq + 1) * TT], ct)
            gst(cv_sp[i, 1, :, tq * TT:(tq + 1) * TT], vt)
            if i == 0 and tq == 0:
                vred(lambda v, ct=ct: v.tensor_copy(out=dbgs[0:1, 60:62], in_=ct[0:1, 0:2]))
                vred(lambda v, vt=vt: v.tensor_copy(out=dbgs[0:1, 62:64], in_=vt[0:1, 0:2]))
            if i == 0 and tq == 3:
                vred(lambda v, ct=ct: v.tensor_copy(out=dbgs[0:1, 36:38], in_=ct[0:1, 510:512]))
                vred(lambda v, vt=vt: v.tensor_copy(out=dbgs[0:1, 38:40], in_=vt[0:1, 510:512]))
            vred(lambda v, ct=ct, vt=vt, i=i: v.tensor_tensor_scan(
                out=qtmp[:, 0:TT], data0=ct, data1=vt, initial=stc[:, i:i + 1],
                op0=ALU.mult, op1=ALU.add))
            vred(lambda v, i=i: v.tensor_copy(out=stc[:, i:i + 1],
                                             in_=qtmp[:, TT - 1:TT]))
            if i == 0 and tq == 1:
                vred(lambda v: v.tensor_copy(out=dbgs[0:1, 33:34], in_=stc[0:1, 0:1]))
            vred(lambda v, ct=ct, i=i: v.tensor_tensor_scan(
                out=qtmp[:, 0:TT], data0=ct, data1=zer[:], initial=stc[:, 12 + i:13 + i],
                op0=ALU.mult, op1=ALU.add))
            vred(lambda v, i=i: v.tensor_copy(out=stc[:, 12 + i:13 + i],
                                             in_=qtmp[:, TT - 1:TT]))
        te_slab[half] = p.cur("TE")
    if limit < 7.3:
        return _finish(nc, p, e, ctx, outp, dbg, dbgs, t0)
    for i in range(NEC):
        vop(lambda v, i=i: v.scalar_tensor_tensor(
            qcol[:, i:i + 1], stc[:, 12 + i:13 + i], pcb[:, 1:2], stc[:, i:i + 1],
            ALU.mult, ALU.add))
    vred(lambda v: v.tensor_copy(out=dbgs[0:1, 56:57], in_=stc[0:1, 0:1]))
    vred(lambda v: v.tensor_copy(out=dbgs[0:1, 57:58], in_=stc[0:1, 12:13]))
    vred(lambda v: v.tensor_copy(out=dbgs[0:1, 58:59], in_=qcol[0:1, 0:1]))
    v = p.cur("VE")
    p.wait("tensor", "VE", v)
    t = top(lambda tn: tn.transpose(pb[7][0:12, 0:128], qcol[:, 0:12], identb[:]))
    p.wait("vector", "TE", t)
    v = vred(lambda v: v.tensor_copy(out=tr12[:], in_=pb[7][0:12, 0:128]))
    p.wait("gpsimd", "VE", v)
    g = gst(ag_i.rearrange("o (i f) -> (o i) f", i=12), tr12[:])
    p.wait("gpsimd", "GD", g)
    if limit < 7.4:
        return _finish(nc, p, e, ctx, outp, dbg, dbgs, t0)
    g = gld(rowsB[0:1, 0:EXP], ag_i)
    p.wait("vector", "GD", g)
    vop(lambda v: v.memset(rowsA[0:1, 0:EXP], 0.0))
    for pr in range(4):
        v = vop(lambda v, pr=pr: v.tensor_single_scalar(
            out=qtmp[0:1, 0:EXP], in_=rowsB[0:1, 0:EXP],
            scalar=pgmb[0:1, pr:pr + 1], op=ALU.mult))
        p.wait("gpsimd", "VE", v)
        g = gst(ar_i[0:1, 0:EXP], qtmp[0:1, 0:EXP])
        p.wait("gpsimd", "GD", g)
        c = p.inc("gpsimd", lambda s: s.collective_compute(
            "AllReduce", ALU.add, replica_groups=[list(range(NCORE))],
            ins=[ar_i[0:1, 0:EXP]], outs=[ar_o[0:1, 0:EXP]]), "CC", 1)
        p.wait("gpsimd", "CC", c)
        g = gld(qtmp[0:1, 0:EXP], ar_o[0:1, 0:EXP])
        p.wait("vector", "GD", g)
        vop(lambda v, pr=pr: v.tensor_single_scalar(
            out=qtmp[0:1, 0:EXP], in_=qtmp[0:1, 0:EXP],
            scalar=pgmb[0:1, 8 + pr:9 + pr], op=ALU.mult))
        vop(lambda v: v.tensor_add(rowsA[0:1, 0:EXP], rowsA[0:1, 0:EXP],
                                   qtmp[0:1, 0:EXP]))
    v = p.cur("VE")
    p.wait("gpsimd", "VE", v)
    g = gst(ag_i, rowsA[0:1, 0:EXP])
    p.wait("gpsimd", "GD", g)
    g = gld(tr12[:], ag_i.rearrange("o (i f) -> (o i) f", i=12))
    p.wait("tensor", "GD", g)
    p.wait("tensor", "VE", p.cur("VE"))
    t = top(lambda tn: tn.transpose(pb[6][:, 0:12], tr12[:], identb[0:12, 0:12]))
    p.wait("vector", "TE", t)
    vred(lambda v: v.tensor_copy(out=stc[:, 16:28], in_=pb[6][:, 0:12]))
    vv = vred(lambda v: v.tensor_scalar(stc[:, 16:28], stc[:, 16:28], pcb[:, 1:2], 0.0,
                                        ALU.add, ALU.add))
    p.wait("tensor", "VE", vv)
    vred(lambda v: v.tensor_copy(out=dbgs[0:1, 8:20], in_=stc[0:1, 16:28]))
    join("vector", "scalar", "gpsimd", "tensor", "sync")

    if limit < 8: return _finish(nc, p, e, ctx, outp, dbg, dbgs, t0, outp_dummy=True)
    for tq in range(NTQ):
        for i in range(NEC):
            half = i % 2
            p.wait("gpsimd", "VE", p.cur("VE"))
            gld(cvb[:, half * 2 * TT:half * 2 * TT + TT],
                cv_sp[i, 0, :, tq * TT:(tq + 1) * TT])
            g = gld(cvb[:, half * 2 * TT + TT:half * 2 * TT + 2 * TT],
                    cv_sp[i, 1, :, tq * TT:(tq + 1) * TT])
            p.wait("vector", "GD", g)
            vred(lambda v, i=i, half=half: v.tensor_tensor_scan(
                out=arena[:, i * TT:(i + 1) * TT],
                data0=cvb[:, half * 2 * TT:half * 2 * TT + TT],
                data1=cvb[:, half * 2 * TT + TT:half * 2 * TT + 2 * TT],
                initial=stc[:, 16 + i:17 + i], op0=ALU.mult, op1=ALU.add))
            vred(lambda v, i=i: v.tensor_copy(out=stc[:, 16 + i:17 + i],
                                             in_=arena[:, (i + 1) * TT - 1:(i + 1) * TT]))
        hsv = p.cur("VE")
        p.wait("tensor", "VE", hsv)
        p.wait("sync", "VE", hsv)
        for o in range(NKC):
            half = o % 2
            l = ld_tiles(half * 2048, dpT[0:NEC, o], 12)
            p.wait("tensor", "LD", l)
            pbk = pb[4 + (o % 4)]
            p.wait("tensor", "VE", p.cur("VE"))
            for i in range(NEC):
                top(lambda tn, i=i, half=half, pbk=pbk: tn.matmul(
                    pbk[:], wsl[:, half * 2048 + i * 128:half * 2048 + (i + 1) * 128],
                    arena[:, i * TT:(i + 1) * TT],
                    start=(i == 0), stop=(i == NEC - 1)))
            t = p.cur("TE")
            p.wait("vector", "TE", t)
            p.wait("vector", "GD", p.cur("GD"))
            g = gld(t1[:], x1_sp[o * 128:(o + 1) * 128, tq * TT:(tq + 1) * TT])
            p.wait("vector", "GD", g)
            v = vop(lambda v, pbk=pbk: v.tensor_add(t0[:], pbk[:], t1[:]))
            p.wait("gpsimd", "VE", v)
            g = gst(x2_sp[o * 128:(o + 1) * 128, tq * TT:(tq + 1) * TT], t0[:])
            p.wait("vector", "GD", g)
            p.wait("sync", "TE", t)
    join("vector", "scalar", "gpsimd", "tensor", "sync")

    def s4_src(k, tq):
        p.wait("gpsimd", "VE", p.cur("VE"))
        g = gld(t0[:], x2_sp[k * 128:(k + 1) * 128, tq * TT:(tq + 1) * TT])
        p.wait("vector", "GD", g)
        p.wait("scalar", "GD", g)
        vop(lambda v: v.engine_nop())

    stats_pass(s4_src, 2, 3)
    join("vector", "scalar", "gpsimd", "tensor")

    if limit < 9: return _finish(nc, p, e, ctx, outp, dbg, dbgs, t0, outp_dummy=True)
    ma = arena
    for tq in range(NTQ):
        for k in range(NKC):
            p.wait("gpsimd", "VE", p.cur("VE"))
            g = gld(t3[:, 0:TT], x2_sp[k * 128:(k + 1) * 128, tq * TT:(tq + 1) * TT])
            p.wait("vector", "GD", g)
            dst = ma[:, k * TPC + tq * TT:k * TPC + (tq + 1) * TT]
            vop(lambda v, tq=tq, d=dst: v.tensor_sub(d, t3[:, 0:TT],
                                                     bcs(2, 3 + tq * TT, TT)))
            vop(lambda v, tq=tq, d=dst: v.tensor_mul(d, d, bcs(3, 3 + tq * TT, TT)))
            vop(lambda v, k=k, d=dst: v.tensor_scalar(
                d, d, ccol("ln3_g", k), ccol("ln3_b", k), ALU.mult, ALU.add))
    join("vector", "tensor", "scalar", "gpsimd", "sync")

    te_slab = [0, 0]
    for h in range(NHC):
        half = h % 2
        if h >= 2:
            p.wait("sync", "TE", te_slab[half])
        l = ld_tiles(half * 2048, m1T[0:NKC, h], 8)
        p.wait("tensor", "LD", l)
        for tq in range(NTQ):
            pbk = pb[(h * NTQ + tq) % 4]
            for k in range(NKC):
                top(lambda tn, k=k, half=half, pbk=pbk, tq=tq: tn.matmul(
                    pbk[:], wsl[:, half * 2048 + k * 128:half * 2048 + (k + 1) * 128],
                    ma[:, k * TPC + tq * TT:k * TPC + (tq + 1) * TT],
                    start=(k == 0), stop=(k == NKC - 1)))
            t = p.cur("TE")
            p.wait("scalar", "TE", t)
            p.wait("scalar", "GD", p.cur("GD"))
            s = sop(lambda sc, h=h, pbk=pbk: sc.activation(
                out=t0[:], in_=pbk[:], func=AF.Gelu, bias=mb1b[:, h:h + 1]))
            p.wait("gpsimd", "SE", s)
            if h == 0 and tq == 0:
                p.wait("vector", "SE", s)
                vred(lambda v: v.tensor_copy(out=dbgs[0:1, 34:36], in_=t0[0:1, 0:2]))
            gst(m2_sp[h, :, tq * TT:(tq + 1) * TT], t0[:])
            p.wait("tensor", "SE", p.cur("SE"))
        te_slab[half] = p.cur("TE")
    join("vector", "scalar", "gpsimd", "tensor", "sync")

    if limit < 10: return _finish(nc, p, e, ctx, outp, dbg, dbgs, t0, outp_dummy=True)
    def q2_datagen(i, dst):
        h, tq = i % NHC, i // NHC
        p.wait("gpsimd", "SE", p.cur("SE"))
        g = gld(dst, ge_sp[h, :, tq * TT:(tq + 1) * TT])
        p.wait("scalar", "GD", g)
        s = sop(lambda s, d=dst: s.activation(out=d, in_=d, func=AF.Abs))
        p.wait("vector", "SE", s)

    qpath(1024, Q2_PLO, Q2_PHI, M2, q2_datagen, NHC * NTQ, sbase=160, scol=46)
    vred(lambda v: v.tensor_copy(out=dbgs[0:1, 3:6], in_=qrow[0:1, 160:163]))
    join("vector", "scalar", "gpsimd")

    for h in range(NHC):
        for tq in range(NTQ):
            p.wait("gpsimd", "VE", p.cur("VE"))
            g = gld(t0[:], ge_sp[h, :, tq * TT:(tq + 1) * TT])
            p.wait("vector", "GD", g)
            vop(lambda v: v.tensor_single_scalar(out=t1[:], in_=t0[:], scalar=qcol[:, 46:47], op=ALU.mult))
            vop(lambda v: v.tensor_copy(out=msk[:], in_=t1[:]))
            vop(lambda v: v.tensor_copy(out=t1[:], in_=msk[:]))
            vop(lambda v: v.tensor_scalar_max(t1[:], t1[:], -128.0))
            v = vop(lambda v: v.tensor_scalar_min(t1[:], t1[:], 127.0))
            if h == 0 and tq == 0:
                vred(lambda v: v.tensor_copy(out=dbgs[0:1, 28:30], in_=t1[0:1, 0:2]))
            p.wait("gpsimd", "VE", v)
            g = gst(q_sp[h, :, tq * TT:(tq + 1) * TT], t1[:])
            p.wait("vector", "GD", g)
    join("vector", "gpsimd")

    if limit < 11: return _finish(nc, p, e, ctx, outp, dbg, dbgs, t0, outp_dummy=True)
    for tq in range(NTQ):
        p.wait("tensor", "VE", p.cur("VE"))
        p.wait("tensor", "SE", p.cur("SE"))
        te_mark = [0] * NHC
        for h in range(NHC):
            half = h % 2
            if h >= 2:
                p.wait("gpsimd", "TE", te_mark[h - 2])
                p.wait("sync", "TE", te_mark[h - 2])
            g = gld(cvb[:, half * TT:half * TT + TT], m2_sp[h, :, tq * TT:(tq + 1) * TT])
            l = ld_tiles(half * 2048, m2T[h, 0:NKC], 8)
            p.wait("tensor", "LD", l)
            p.wait("tensor", "GD", g)
            for o in range(NKC):
                top(lambda tn, o=o, h=h, half=half: tn.matmul(
                    pb[o][:], wsl[:, half * 2048 + o * 128:half * 2048 + (o + 1) * 128],
                    cvb[:, half * TT:half * TT + TT],
                    start=(h == 0), stop=(h == NHC - 1)))
            te_mark[h] = p.cur("TE")
        t = p.cur("TE")
        p.wait("scalar", "TE", t)
        for o in range(NKC):
            p.wait("scalar", "GD", p.cur("GD"))
            s = sop(lambda sc, o=o: sc.activation(
                out=t0[:], in_=pb[o][:], func=AF.Identity, bias=ccol("mb2", o)))
            p.wait("vector", "SE", s)
            g = gld(t1[:], x2_sp[o * 128:(o + 1) * 128, tq * TT:(tq + 1) * TT])
            p.wait("vector", "GD", g)
            v = vop(lambda v: v.tensor_add(t0[:], t0[:], t1[:]))
            if o == 0 and tq == 0:
                vred(lambda v: v.tensor_copy(out=dbgs[0:1, 36:38], in_=t0[0:1, 0:2]))
            p.wait("gpsimd", "VE", v)
            gst(ao_sp[o * 128:(o + 1) * 128, tq * TT:(tq + 1) * TT], t0[:])
        p.wait("tensor", "SE", p.cur("SE"))
        p.wait("tensor", "GD", p.cur("GD"))
    join("vector", "scalar", "gpsimd", "tensor", "sync")

    if limit < 12: return _finish(nc, p, e, ctx, outp, dbg, dbgs, t0, outp_dummy=True)
    for tq in range(NTQ):
        p.wait("tensor", "VE", p.cur("VE"))
        te_mark = [0] * NHC
        for h in range(NHC):
            half = h % 2
            if h >= 2:
                p.wait("gpsimd", "TE", te_mark[h - 2])
                p.wait("sync", "TE", te_mark[h - 2])
            g = gld(cvb[:, half * TT:half * TT + TT], q_sp[h, :, tq * TT:(tq + 1) * TT])
            l = ld_tiles(half * 2048, f2T[h, 0:NKC], 8)
            p.wait("gpsimd", "LD", l)
            p.wait("gpsimd", "GD", g)
            d = wsl[:, half * 2048:half * 2048 + 8 * 128]
            gop(lambda g, d=d: g.tensor_single_scalar(out=d, in_=d, scalar=qcol[:, 53:54], op=ALU.mult))
            gop(lambda g, d=d: g.tensor_copy(out=wint[:, 0:1024], in_=d))
            gop(lambda g, d=d: g.tensor_copy(out=d, in_=wint[:, 0:1024]))
            gop(lambda g, d=d: g.tensor_scalar_max(d, d, -1.0))
            gop(lambda g, d=d: g.tensor_scalar_min(d, d, 1.0))
            gop(lambda g, d=d: g.tensor_single_scalar(out=d, in_=d, scalar=qcol[:, 51:52], op=ALU.mult))
            gq = gop(lambda g, half=half: g.tensor_single_scalar(
                out=cvb[:, half * TT:half * TT + TT],
                in_=cvb[:, half * TT:half * TT + TT],
                scalar=qcol[:, 47:48], op=ALU.mult))
            p.wait("tensor", "GE", gq)
            for o in range(NKC):
                top(lambda tn, o=o, h=h, half=half: tn.matmul(
                    pb[o][:], wsl[:, half * 2048 + o * 128:half * 2048 + (o + 1) * 128],
                    cvb[:, half * TT:half * TT + TT],
                    start=(h == 0), stop=(h == NHC - 1)))
            te_mark[h] = p.cur("TE")
        t = p.cur("TE")
        p.wait("vector", "TE", t)
        for o in range(NKC):
            p.wait("vector", "ST", p.cur("ST"))
            p.wait("gpsimd", "VE", p.cur("VE"))
            g = gld(t1[:], ao_sp[o * 128:(o + 1) * 128, tq * TT:(tq + 1) * TT])
            p.wait("sync", "VE", p.cur("VE"))
            l = ld(t2[:], xT[o * 128:(o + 1) * 128, tq * TT:(tq + 1) * TT])
            p.wait("vector", "GD", g)
            p.wait("vector", "LD", l)
            vop(lambda v, o=o: v.tensor_add(t0[:], pb[o][:], t1[:]))
            if o == 0 and tq == 0:
                vred(lambda v, o=o: v.tensor_copy(out=dbgs[0:1, 30:31], in_=pb[o][0:1, 0:1]))
                vred(lambda v: v.tensor_copy(out=dbgs[0:1, 31:32], in_=t1[0:1, 0:1]))
            v = vop(lambda v: v.tensor_add(t0[:], t0[:], t2[:]))
            p.wait("sync", "VE", v)
            p.inc("sync", lambda s, o=o, tq=tq: s.dma_start(
                out=outp[o * 128:(o + 1) * 128, tq * TT:(tq + 1) * TT], in_=t0[:]),
                "ST", 16)
        p.wait("gpsimd", "TE", p.cur("TE"))
    v = p.cur("VE")
    p.wait("sync", "VE", v)
    st_total = p.inc("sync", lambda s: s.dma_start(out=dbg, in_=dbgs[:]), "ST", 16)
    p.wait("gpsimd", "ST", st_total)

    blk = e(nc.Block())
    for eng_name, deco in (("sync", blk.sync), ("scalar", blk.scalar),
                           ("vector", blk.vector), ("gpsimd", blk.gpsimd),
                           ("tensor", blk.tensor)):
        ops = list(p.ops[eng_name])

        def make_body(ops):
            def body(eng):
                for op in ops:
                    op(eng)
            return body
        deco(make_body(ops))
    ctx.close()
    return nc



def _finish(nc, p, e, ctx, outp, dbg, dbgs, t0, outp_dummy=False):
    v = p.cur("VE")
    p.wait("sync", "VE", v)
    st = p.inc("sync", lambda s: s.dma_start(out=dbg, in_=dbgs[:]), "ST", 16)
    p.wait("gpsimd", "ST", st)
    blk = e(nc.Block())
    for eng_name, deco in (("sync", blk.sync), ("scalar", blk.scalar),
                           ("vector", blk.vector), ("gpsimd", blk.gpsimd),
                           ("tensor", blk.tensor)):
        ops = list(p.ops[eng_name])

        def make_body(ops):
            def body(eng):
                for op in ops:
                    op(eng)
            return body
        deco(make_body(ops))
    ctx.close()
    return nc

_NC_CACHE = None


def kernel(**inputs):
    global _NC_CACHE
    x = np.asarray(inputs["L0_attn_x"], np.float32)
    xf = np.ascontiguousarray(x.reshape(B * T, D))

    names = ["L0_attn_ln1_g", "L0_attn_ln1_b", "L0_attn_pw_b", "L0_attn_ln2_g",
             "L0_attn_ln2_b", "L0_attn_ln3_g", "L0_attn_ln3_b", "L0_ffn_ln_g",
             "L0_ffn_ln_b", "L0_attn_mlp_b2", "L0_attn_dw_b"]
    colsm = np.zeros((128, 96), np.float32)
    for idx, nm in enumerate(names):
        colsm[:, idx * 8:(idx + 1) * 8] = _col8(np.asarray(inputs[nm], np.float32))

    fb = np.asarray(inputs["L0_attn_f_b"], np.float32)
    fbc = np.zeros((128, 24), np.float32)
    for i in range(12):
        fbc[:, i] = fb[i * 128:(i + 1) * 128]
        fbc[:, 12 + i] = fb[EXP + i * 128:EXP + (i + 1) * 128]

    shared = {
        "dwt": np.ascontiguousarray(
            np.asarray(inputs["L0_attn_dw_w"], np.float32).reshape(D, 4)
            .reshape(NKC, 128, 4).transpose(1, 0, 2).reshape(128, NKC * 4)),
        "cols": np.ascontiguousarray(colsm),
        "fb_c": np.ascontiguousarray(fbc),
        "mb1_c": np.ascontiguousarray(
            np.asarray(inputs["L0_attn_mlp_b1"], np.float32).reshape(32, 128).T),
        "pwT": _tile4(np.ascontiguousarray(np.asarray(inputs["L0_attn_pw_w"], np.float32).T)),
        "fwT": _tile4(np.ascontiguousarray(np.asarray(inputs["L0_attn_f_w"], np.float32).T)),
        "dpT": _tile4(np.ascontiguousarray(np.asarray(inputs["L0_attn_dp_w"], np.float32).T)),
        "m1T": _tile4(np.ascontiguousarray(np.asarray(inputs["L0_attn_mlp_w1"], np.float32).T)),
        "m2T": _tile4(np.ascontiguousarray(np.asarray(inputs["L0_attn_mlp_w2"], np.float32).T)),
        "f1T": _tile4(np.ascontiguousarray(np.asarray(inputs["L0_ffn_w1"], np.float32).T)),
        "f2T": _tile4(np.ascontiguousarray(np.asarray(inputs["L0_ffn_w2"], np.float32).T)),
    }

    in_maps = []
    for c in range(NCORE):
        lo = c * TPC
        m = dict(shared)
        m["xT"] = np.ascontiguousarray(xf[lo:lo + TPC].T)
        xhp = np.zeros((128, 128), np.float32)
        if c % 2 == 1:
            hl = xf[lo - 3:lo].T.reshape(NKC, 128, 3).transpose(1, 0, 2).reshape(128, 24)
            xhp[:, 0:24] = hl
            hm, h0 = 1.0, 0.0
        else:
            hm, h0 = 0.0, 0.5
        m["xh"] = xhp
        pc = np.zeros((128, 4), np.float32)
        pc[:, 0] = hm
        pc[:, 1] = h0
        pc[c, 3] = 1.0
        if c % 2 == 1:
            pc[c - 1, 2] = 1.0
        m["percore"] = pc
        m["ident"] = np.eye(128, dtype=np.float32)
        oh = np.zeros((1, 8), np.float32)
        oh[0, c] = 1.0
        m["ohrow"] = oh
        pg = np.zeros((1, 16), np.float32)
        if c % 2 == 0:
            pg[0, c // 2] = 1.0
        else:
            pg[0, 8 + c // 2] = 1.0
        m["pgm"] = pg
        in_maps.append(m)

    if _NC_CACHE is None:
        _NC_CACHE = build_program()
    res = run_bass_kernel_spmd(_NC_CACHE, in_maps, core_ids=list(range(NCORE)))
    outs = [np.asarray(res.results[c]["out"]).T for c in range(NCORE)]
    return np.ascontiguousarray(np.concatenate(outs, axis=0).reshape(B, T, D))


# version 08
# speedup vs baseline: 1.0029x; 1.0029x over previous
import numpy as np
import concourse.bass as bass
import concourse.mybir as mybir
from concourse.bass_utils import run_bass_kernel_spmd

F32 = mybir.dt.float32
I32 = mybir.dt.int32
AF = mybir.ActivationFunctionType
ALU = mybir.AluOpType
AXX = mybir.AxisListType.X

B, T, D = 4, 4096, 1024
NCORE = 8
TPC = (B * T) // NCORE
NKC = D // 128
TT = 512
NTQ = TPC // TT
EXP = 1536
NEC = EXP // 128
HID = 4096
NHC = HID // 128
M1 = float(B * T * D)
M2 = float(B * T * HID)
C995_HI = float(np.float32(0.995))
C995_LO = float(np.float64(0.995) - np.float64(np.float32(0.995)))

Q1_PLO, Q1_PHI = 1.5, 4.0
Q2_PLO, Q2_PHI = 0.25, 4.25
NPIL = 40
DENSE_W = 0.28
NDEN = 28
EXT_W = 0.06
NBIS = 9
TBIS = 8
NTOP = 16
BCW = TPC + 3


def _tile4(w):
    K, M = w.shape
    return np.ascontiguousarray(
        w.reshape(K // 128, 128, M // 128, 128).transpose(0, 2, 1, 3))


def _col8(v):
    return np.ascontiguousarray(v.reshape(-1, 128).T)


class P:
    def __init__(self, nc):
        self.nc = nc
        self.ops = {k: [] for k in ("sync", "scalar", "vector", "gpsimd", "tensor")}
        self.cnt = {}
        self.handles = {}

    def sem(self, name):
        h = self.nc.semaphore(name).__enter__()
        self.cnt[name] = 0
        self.handles[name] = h
        return h

    def emit(self, eng, fn):
        self.ops[eng].append(fn)

    def inc(self, eng, builder, sem, n=1):
        self.cnt[sem] += n
        tgt = self.cnt[sem]
        h = self.handles[sem]
        self.emit(eng, lambda e, b=builder, h=h, n=n: b(e).then_inc(h, n))
        return tgt

    def plain(self, eng, builder):
        self.emit(eng, lambda e, b=builder: b(e))

    def wait(self, eng, sem, val):
        h = self.handles[sem]
        self.emit(eng, lambda e, h=h, v=val: e.wait_ge(h, v))

    def cur(self, sem):
        return self.cnt[sem]


def build_program(limit=99):
    nc = bass.Bass("TRN2", target_bir_lowering=False, debug=False)
    dram = {}

    def din(name, shape, dt=F32):
        dram[name] = nc.dram_tensor(name, shape, dt, kind="ExternalInput").ap()
        return dram[name]

    def dout(name, shape, dt=F32):
        dram[name] = nc.dram_tensor(name, shape, dt, kind="ExternalOutput").ap()
        return dram[name]

    def dint(name, shape, dt=F32, shared=False):
        if shared:
            dram[name] = nc.dram_tensor(name, shape, dt, addr_space="Shared").ap()
        else:
            dram[name] = nc.dram_tensor(name, shape, dt).ap()
        return dram[name]

    xT = din("xT", [D, TPC])
    xh = din("xh", [128, 128])
    dwt = din("dwt", [128, 4 * NKC])
    cols = din("cols", [128, 8 * 12])
    fb_c = din("fb_c", [128, 24])
    mb1_c = din("mb1_c", [128, 32])
    percore = din("percore", [128, 4])
    ident = din("ident", [128, 128])
    ohrow = din("ohrow", [1, 8])
    pgm = din("pgm", [1, 16])
    pwT = din("pwT", [NKC, NKC, 128, 128])
    fwT = din("fwT", [NKC, 2 * NEC, 128, 128])
    dpT = din("dpT", [NEC, NKC, 128, 128])
    m1T = din("m1T", [NKC, NHC, 128, 128])
    m2T = din("m2T", [NHC, NKC, 128, 128])
    f1T = din("f1T", [NKC, NHC, 128, 128])
    f2T = din("f2T", [NHC, NKC, 128, 128])

    outp = dout("out", [D, TPC])
    dbg = dout("dbg", [1, 64])
    dbg2 = dout("dbg2", [128, 8])

    agr_i = dint("agr_i", [NCORE, EXP])
    agr_o = dint("agr_o", [NCORE, EXP], shared=True)
    xn_sp = dint("xn_sp", [D, TPC + 3])
    x1_sp = dint("x1_sp", [D, TPC])
    x2_sp = dint("x2_sp", [D, TPC])
    ao_sp = dint("ao_sp", [D, TPC])
    cv_sp = dint("cv_sp", [NEC, 2, 128, TPC])
    ge_sp = dint("ge_sp", [NHC, 128, TPC])
    m2_sp = dint("m2_sp", [NHC, 128, TPC])
    q_sp = dint("q_sp", [NHC, 128, TPC])
    ag_i = dint("ag_i", [1, EXP])
    ag_o = dint("ag_o", [NCORE, EXP])
    ar_i = dint("ar_i", [1, 2048])
    ar_o = dint("ar_o", [1, 2048], shared=True)
    arm_i = dint("arm_i", [1, 8])
    arm_o = dint("arm_o", [1, 8], shared=True)
    tb = dint("tb", [1, 4096])

    COLS = {n: i for i, n in enumerate(
        ["ln1_g", "ln1_b", "pw_b", "ln2_g", "ln2_b", "ln3_g", "ln3_b",
         "ffn_g", "ffn_b", "mb2", "dw_b", "spare"])}

    from contextlib import ExitStack
    ctx = ExitStack()
    e = ctx.enter_context

    def sbt(name, shape, dt=F32):
        return e(nc.sbuf_tensor(name, shape, dt))

    arena = sbt("arena", [128, 16384])
    cvb = sbt("cvb", [128, 4 * TT])
    wsl = sbt("wsl", [128, 2 * 16 * 128])
    bc = sbt("bc", [128, 4 * BCW])
    rowsA = sbt("rowsA", [1, BCW])
    rowsB = sbt("rowsB", [1, BCW])
    colsb = sbt("colsb", [128, 8 * 12])
    fbb = sbt("fbb", [128, 48])
    mb1b = sbt("mb1b", [128, 32])
    dwb = sbt("dwb", [128, 4 * NKC])
    pcb = sbt("pcb", [128, 4])
    identb = sbt("identb", [128, 128])
    ohrowb = sbt("ohrowb", [1, 8])
    pgmb = sbt("pgmb", [1, 16])
    onesb = sbt("onesb", [128, 130])
    t0 = sbt("t0", [128, TT])
    t1 = sbt("t1", [128, TT])
    t2 = sbt("t2", [128, TT])
    t3 = sbt("t3", [128, TT + 3])
    zer = sbt("zer", [128, TT])
    stc = sbt("stc", [128, 32])
    qbuf = sbt("qbuf", [128, 2048])
    qtmp = sbt("qtmp", [128, 4096])
    qacc = sbt("qacc", [128, 64])
    qrow = sbt("qrow", [1, 256])
    qcol = sbt("qcol", [128, 64])
    pil = sbt("pil", [128, 1024])
    msk = sbt("msk", [128, TT], I32)
    wint = sbt("wint", [128, 1024], I32)
    agb = sbt("agb", [NCORE, EXP])
    dbgs = sbt("dbgs", [1, 64])
    tr12 = sbt("tr12", [12, 128])
    h1h = sbt("h1h", [128, 24])

    pb = [e(nc.psum_tensor(f"pb{i}", [128, TT], F32)) for i in range(8)]

    p = P(nc)
    for s in ("LD", "GD", "TE", "VE", "SE", "GE", "CC", "ST"):
        p.sem(s)

    def ld(dst_ap, src_ap):
        return p.inc("sync", lambda s, d=dst_ap, r=src_ap: s.dma_start(out=d, in_=r), "LD", 16)

    def gld(dst_ap, src_ap):
        return p.inc("gpsimd", lambda s, d=dst_ap, r=src_ap: s.dma_start(out=d, in_=r), "GD", 16)

    gst = gld

    def ld_tiles(dst_base_col, src3, n):
        l = 0
        for a in range(n):
            l = ld(wsl[:, dst_base_col + a * 128:dst_base_col + (a + 1) * 128], src3[a])
        return l

    def vop(builder):
        n = p.inc("vector", builder, "VE", 1)
        p.wait("vector", "VE", n)
        return n

    def sop(builder):
        return p.inc("scalar", builder, "SE", 1)

    def gop(builder):
        n = p.inc("gpsimd", builder, "GE", 1)
        p.wait("gpsimd", "GE", n)
        return n

    def top(builder):
        return p.inc("tensor", builder, "TE", 1)

    def vred(builder):
        n = p.inc("vector", builder, "VE", 1)
        p.wait("vector", "VE", n)
        return n

    def join(*engs):
        engs = engs or ("sync", "scalar", "vector", "gpsimd", "tensor")
        marks = {}
        for eng, sem in (("sync", "LD"), ("gpsimd", "GD"), ("tensor", "TE"),
                         ("vector", "VE"), ("scalar", "SE")):
            marks[sem] = p.cur(sem)
        marks["GE"] = p.cur("GE")
        marks["ST"] = p.cur("ST")
        for eng in engs:
            for sem in ("LD", "GD", "TE", "VE", "SE", "GE", "ST"):
                if marks[sem] > 0:
                    p.wait(eng, sem, marks[sem])

    def bcast_row(dst_tile, row_ap, n):
        o = 0
        while o < n:
            m = min(TT, n - o)
            p.wait("tensor", "VE", p.cur("VE"))
            t = top(lambda tn, row=row_ap, o=o, m=m: tn.matmul(
                pb[7][:, 0:m], onesb[0:1, 2:130], row[0:1, o:o + m],
                start=True, stop=True))
            p.wait("vector", "TE", t)
            vop(lambda v, o=o, m=m, d=dst_tile: v.tensor_copy(
                out=d[:, o:o + m], in_=pb[7][:, 0:m]))
            o += m

    def pfold(dst_row, src_cols, n, op):
        if op == ALU.add:
            v = p.cur("VE")
            p.wait("tensor", "VE", v)
            t = top(lambda tn, n=n: tn.matmul(pb[7][0:1, 0:n], onesb[:, 0:1],
                                              src_cols[:, 0:n], start=True, stop=True))
            p.wait("vector", "TE", t)
            vred(lambda v, n=n, d=dst_row: v.tensor_copy(out=d, in_=pb[7][0:1, 0:n]))
        else:
            for j in range(n):
                v = p.cur("VE")
                p.wait("tensor", "VE", v)
                t = top(lambda tn, j=j: tn.transpose(
                    pb[7][0:1, 0:128], src_cols[:, j:j + 1], identb[:]))
                p.wait("vector", "TE", t)
                vred(lambda v, j=j, d=dst_row: v.reduce_max(
                    d[0:1, j:j + 1] if False else d.tensor[d.offset_unused] if False else d,
                    pb[7][0:1, 0:128], AXX) if False else v.engine_nop())
                vred(lambda v, j=j: v.tensor_reduce(
                    qtmp[0:1, 32 + j:33 + j], pb[7][0:1, 0:128], AXX, ALU.max))
            vred(lambda v, n=n, d=dst_row: v.tensor_copy(out=d, in_=qtmp[0:1, 32:32 + n]))
        return p.cur("VE")

    def allreduce(n, op, src_row, dst_row):
        n8 = (n + 7) // 8 * 8
        bi, bo = (ar_i, ar_o) if op == ALU.add else (arm_i, arm_o)
        p.wait("gpsimd", "VE", p.cur("VE"))
        g = gst(bi[0:1, 0:n], src_row)
        p.wait("gpsimd", "GD", g)
        c = p.inc("gpsimd", lambda s, n8=n8, op=op, bi=bi, bo=bo: s.collective_compute(
            "AllReduce", op, replica_groups=[list(range(NCORE))],
            ins=[bi[0:1, 0:n8]], outs=[bo[0:1, 0:n8]]), "CC", 1)
        p.wait("gpsimd", "CC", c)
        g = gld(dst_row, bo[0:1, 0:n])
        p.wait("vector", "GD", g)

    def bcs(slot, lo, m):
        return bc[:, slot * BCW + lo:slot * BCW + lo + m]

    def ccol(name, k):
        i = COLS[name]
        return colsb[:, i * 8 + k:i * 8 + k + 1]

    ld(colsb[:], cols)
    ld(fbb[:, 0:24], fb_c)
    ld(mb1b[:], mb1_c)
    ld(dwb[:], dwt)
    ld(pcb[:], percore)
    ld(identb[:], ident)
    ld(ohrowb[:], ohrow)
    ld(pgmb[:], pgm)
    la = p.cur("LD")
    for eng in ("vector", "scalar", "gpsimd", "tensor"):
        p.wait(eng, "LD", la)
    vop(lambda v: v.memset(onesb[:], 1.0))
    vop(lambda v: v.memset(zer[:], 0.0))
    vop(lambda v: v.memset(dbgs[:], 0.0))
    vop(lambda v: v.memset(qrow[:], 0.0))
    vop(lambda v: v.memset(qtmp[:], 0.0))
    vop(lambda v: v.memset(qcol[:], 0.0))
    vop(lambda v: v.memset(qacc[:], 0.0))
    vop(lambda v: v.tensor_scalar_mul(fbb[:, 24:48], fbb[:, 0:24], -1.0))
    g = gst(ar_i[0:1, 0:64], qrow[0:1, 0:64])
    g = gst(arm_i[0:1, 0:8], qrow[0:1, 0:8])
    p.wait("gpsimd", "GD", g)
    join()

    if limit < 1: return _finish(nc, p, e, ctx, outp, dbg, dbgs, t0, outp_dummy=True)
    for wi, wt in enumerate(("f1T", "f2T")):
        src = dram[wt].rearrange("a b p f -> (a b) p f")
        pbank = pb[6 + wi]
        se_mark = [0, 0]
        for t in range(0, 256, 8):
            half = (t // 8) % 2
            base = half * 2048
            p.wait("sync", "SE", se_mark[half])
            l = ld_tiles(base, src[t:t + 8], 8)
            p.wait("scalar", "LD", l)
            p.wait("scalar", "TE", p.cur("TE"))
            sab = sop(lambda sc, base=base: sc.activation(
                out=arena[:, 0:1024], in_=wsl[:, base:base + 1024], func=AF.Abs))
            se_mark[half] = sab
            p.wait("tensor", "SE", sab)
            top(lambda tn, t=t, pbank=pbank: tn.matmul(
                pbank[0:1, :], onesb[:, 0:1], arena[:, 0:512],
                start=(t == 0), stop=False))
            top(lambda tn, t=t, pbank=pbank: tn.matmul(
                pbank[0:1, :], onesb[:, 0:1], arena[:, 512:1024],
                start=False, stop=(t == 248)))
        t_ = p.cur("TE")
        p.wait("vector", "TE", t_)
        vred(lambda v, pbank=pbank: v.tensor_copy(out=qtmp[0:1, 0:512],
                                                 in_=pbank[0:1, :]))
        vred(lambda v, wi=wi: v.tensor_reduce(qrow[0:1, 40 + wi:41 + wi],
                                              qtmp[0:1, 0:512], AXX, ALU.add))
    vred(lambda v: v.tensor_copy(out=dbgs[0:1, 20:22], in_=qrow[0:1, 40:42]))
    vred(lambda v: v.tensor_copy(out=dbgs[0:1, 24:26], in_=qrow[0:1, 40:42]))
    vred(lambda v: v.tensor_scalar(qrow[0:1, 40:42], qrow[0:1, 40:42],
                                  1.0 / (HID * D), 1e-5, ALU.mult, ALU.add))
    vred(lambda v: v.reciprocal(qrow[0:1, 42:44], qrow[0:1, 40:42]))
    bcast_row(qcol[:, 50:52], qrow[0:1, 40:42], 2)
    bcast_row(qcol[:, 52:54], qrow[0:1, 42:44], 2)
    vred(lambda v: v.tensor_copy(out=dbgs[0:1, 6:8], in_=qrow[0:1, 40:42]))
    join()

    def stats_pass(src_fn, slot_mu, slot_rs, with_halo_fn=None, n=TPC):
        for tq in range(NTQ):
            for k in range(NKC):
                src_fn(k, tq)
                v = p.cur("VE")
                p.wait("scalar", "VE", v)
                s = sop(lambda s: s.square(out=t1[:], in_=t0[:]))
                p.wait("tensor", "SE", s)
                p.wait("tensor", "VE", v)
                top(lambda tn, k=k: tn.matmul(pb[6][0:1, :], onesb[:, 0:1], t0[:],
                                              start=(k == 0), stop=(k == NKC - 1)))
                t = top(lambda tn, k=k: tn.matmul(pb[7][0:1, :], onesb[:, 0:1], t1[:],
                                                  start=(k == 0), stop=(k == NKC - 1)))
                p.wait("vector", "TE", t)
                p.wait("scalar", "TE", t)
            vred(lambda v, tq=tq: v.tensor_copy(
                out=rowsA[0:1, 3 + tq * TT:3 + (tq + 1) * TT], in_=pb[6][0:1, :]))
            vv = vred(lambda v, tq=tq: v.tensor_copy(
                out=rowsB[0:1, 3 + tq * TT:3 + (tq + 1) * TT], in_=pb[7][0:1, :]))
            p.wait("tensor", "VE", vv)
        if with_halo_fn is not None:
            for k in range(NKC):
                with_halo_fn(k)
                v = p.cur("VE")
                p.wait("scalar", "VE", v)
                s = sop(lambda s: s.square(out=t1[:, 0:3], in_=t0[:, 0:3]))
                p.wait("tensor", "SE", s)
                p.wait("tensor", "VE", v)
                top(lambda tn, k=k: tn.matmul(pb[6][0:1, 0:3], onesb[:, 0:1], t0[:, 0:3],
                                              start=(k == 0), stop=(k == NKC - 1)))
                t = top(lambda tn, k=k: tn.matmul(pb[7][0:1, 0:3], onesb[:, 0:1], t1[:, 0:3],
                                                  start=(k == 0), stop=(k == NKC - 1)))
                p.wait("vector", "TE", t)
                p.wait("scalar", "TE", t)
            vred(lambda v: v.tensor_copy(out=rowsA[0:1, 0:3], in_=pb[6][0:1, 0:3]))
            vv = vred(lambda v: v.tensor_copy(out=rowsB[0:1, 0:3], in_=pb[7][0:1, 0:3]))
            p.wait("tensor", "VE", vv)
        w = BCW if with_halo_fn is not None else n + 3
        lo = 0 if with_halo_fn is not None else 3
        m = w - lo
        vred(lambda v, lo=lo, m=m: v.tensor_scalar_mul(
            rowsA[0:1, lo:lo + m], rowsA[0:1, lo:lo + m], 1.0 / D))
        vred(lambda v, lo=lo, m=m: v.tensor_scalar_mul(
            rowsB[0:1, lo:lo + m], rowsB[0:1, lo:lo + m], 1.0 / D))
        vred(lambda v, lo=lo, m=m: v.tensor_tensor(
            qtmp[0:1, 0:m], rowsA[0:1, lo:lo + m], rowsA[0:1, lo:lo + m], ALU.mult))
        vred(lambda v, lo=lo, m=m: v.tensor_sub(
            rowsB[0:1, lo:lo + m], rowsB[0:1, lo:lo + m], qtmp[0:1, 0:m]))
        vv = vred(lambda v, lo=lo, m=m: v.tensor_scalar_add(
            rowsB[0:1, lo:lo + m], rowsB[0:1, lo:lo + m], 1e-5))
        p.wait("scalar", "VE", vv)
        s = sop(lambda s, lo=lo, m=m: s.activation(
            out=rowsB[0:1, lo:lo + m], in_=rowsB[0:1, lo:lo + m], func=AF.Sqrt))
        p.wait("vector", "SE", s)
        vv = vred(lambda v, lo=lo, m=m: v.reciprocal(
            rowsB[0:1, lo:lo + m], rowsB[0:1, lo:lo + m]))
        p.wait("tensor", "VE", vv)
        p.wait("scalar", "VE", vv)
        bcast_row(bc[:, slot_mu * BCW + lo:slot_mu * BCW + w], rowsA[0:1, lo:w], m)
        bcast_row(bc[:, slot_rs * BCW + lo:slot_rs * BCW + w], rowsB[0:1, lo:w], m)
        return p.cur("VE")

    if limit < 2: return _finish(nc, p, e, ctx, outp, dbg, dbgs, t0, outp_dummy=True)
    xa = arena
    for k in range(NKC):
        ld(xa[:, k * TPC:(k + 1) * TPC], xT[k * 128:(k + 1) * 128, :])
    lh = ld(qtmp[:, 0:24].rearrange("p (k f) -> k p f", f=3) if False else
            qtmp[:, 0:24], None) if False else None
    ld(qbuf[:, 0:128], xh)
    lx = p.cur("LD")
    p.wait("scalar", "LD", lx)
    p.wait("vector", "LD", lx)

    def s1_src(k, tq):
        vop(lambda v, k=k, tq=tq: v.tensor_copy(
            out=t0[:], in_=xa[:, k * TPC + tq * TT:k * TPC + (tq + 1) * TT]))

    def s1_halo(k):
        vop(lambda v, k=k: v.tensor_copy(out=t0[:, 0:3], in_=qbuf[:, k * 3:(k + 1) * 3]))

    stats_pass(s1_src, 0, 1)
    vred(lambda v: v.tensor_copy(out=dbgs[0:1, 26:29], in_=rowsA[0:1, 3:6]))
    vred(lambda v: v.tensor_copy(out=dbgs[0:1, 29:32], in_=rowsB[0:1, 3:6]))
    join("vector", "scalar", "gpsimd")

    if limit < 3: return _finish(nc, p, e, ctx, outp, dbg, dbgs, t0, outp_dummy=True)
    def xn_make(k, tq):
        lo = tq * TT
        vop(lambda v, k=k, lo=lo: v.tensor_sub(
            t0[:], xa[:, k * TPC + lo:k * TPC + lo + TT], bcs(0, 3 + lo, TT)))
        return vop(lambda v, lo=lo: v.tensor_mul(t0[:], t0[:], bcs(1, 3 + lo, TT)))

    def s2_src(k, tq):
        v = xn_make(k, tq)
        p.wait("gpsimd", "VE", v)
        g = gst(xn_sp[k * 128:(k + 1) * 128, 3 + tq * TT:3 + (tq + 1) * TT], t0[:])
        vop(lambda v, k=k, tq=tq: v.tensor_copy(
            out=pil[:, (tq * NKC + k) * 8:(tq * NKC + k) * 8 + 8], in_=t0[:, 0:TT:64]))
        p.wait("vector", "GD", g)

    def s2_halo(k):
        vop(lambda v, k=k: v.tensor_sub(t0[:, 0:3], qbuf[:, k * 3:(k + 1) * 3],
                                        bcs(0, 0, 3)))
        v = vop(lambda v: v.tensor_mul(t0[:, 0:3], t0[:, 0:3], bcs(1, 0, 3)))
        p.wait("gpsimd", "VE", v)
        g = gst(xn_sp[k * 128:(k + 1) * 128, 0:3], t0[:, 0:3])
        p.wait("vector", "GD", g)

    stats_pass(s2_src, 2, 3)
    vred(lambda v: v.tensor_copy(out=dbgs[0:1, 32:35], in_=rowsA[0:1, 3:6]))
    vred(lambda v: v.tensor_copy(out=dbgs[0:1, 35:38], in_=rowsB[0:1, 3:6]))
    vred(lambda v: v.tensor_copy(out=dbgs[0:1, 38:40], in_=bc[0:1, 2 * BCW + 3:2 * BCW + 5]))
    join("vector", "scalar", "gpsimd")

    def h1f_tile(k, tq, dst):
        lo = tq * TT
        p.wait("gpsimd", "VE", p.cur("VE"))
        g = gld(t3[:, 0:TT], xn_sp[k * 128:(k + 1) * 128, 3 + lo:3 + lo + TT])
        p.wait("vector", "GD", g)
        vop(lambda v, lo=lo, d=dst: v.tensor_sub(d, t3[:, 0:TT], bcs(2, 3 + lo, TT)))
        vop(lambda v, lo=lo, d=dst: v.tensor_mul(d, d, bcs(3, 3 + lo, TT)))
        return vop(lambda v, k=k, d=dst: v.tensor_scalar(
            d, d, ccol("ffn_g", k), ccol("ffn_b", k), ALU.mult, ALU.add))

    def qpath(pilw, plo, phi, Mv, datagen, ntiles, sbase, scol):
        W = ntiles * NTOP
        vop(lambda v: v.memset(qacc[:, 0:NPIL], 0.0))
        for j in range(NPIL):
            th = plo + (phi - plo) * (j + 1) / (NPIL + 1)
            vop(lambda v, th=th, j=j, w=pilw: v.tensor_scalar(
                qtmp[:, 0:w], pil[:, 0:w], float(th), qacc[:, j:j + 1],
                ALU.is_gt, ALU.add, accum_out=qacc[:, j:j + 1]))
        pass
        pfold(qrow[0:1, 0:NPIL], qacc[:, 0:NPIL], NPIL, ALU.add)
        allreduce(NPIL, ALU.add, qrow[0:1, 0:NPIL], qrow[0:1, 0:NPIL])
        vred(lambda v: v.tensor_copy(out=dbgs[0:1, 60:64], in_=qrow[0:1, 12:16]))
        ptail = (Mv / 64.0) * 0.005
        for j in range(NPIL):
            th = plo + (phi - plo) * (j + 1) / (NPIL + 1)
            vred(lambda v, th=th, j=j: v.memset(qtmp[0:1, 64 + j:65 + j], float(th)))
        vred(lambda v: v.tensor_single_scalar(qtmp[0:1, 0:NPIL], qrow[0:1, 0:NPIL],
                                             float(ptail), ALU.is_gt))
        vred(lambda v: v.tensor_mul(qtmp[0:1, 0:NPIL], qtmp[0:1, 0:NPIL],
                                   qtmp[0:1, 64:64 + NPIL]))
        vred(lambda v: v.tensor_reduce(qrow[0:1, 60:61], qtmp[0:1, 0:NPIL], AXX, ALU.max))
        vred(lambda v: v.tensor_scalar_max(qrow[0:1, 60:61], qrow[0:1, 60:61], float(plo)))
        vred(lambda v: v.tensor_scalar_add(qrow[0:1, 61:62], qrow[0:1, 60:61], -DENSE_W))
        vred(lambda v: v.tensor_scalar_add(qrow[0:1, 62:63], qrow[0:1, 60:61], DENSE_W))
        vred(lambda v: v.tensor_scalar_add(qrow[0:1, 63:64], qrow[0:1, 60:61], EXT_W))
        vred(lambda v: v.tensor_copy(out=dbgs[0:1, 56:60], in_=qrow[0:1, 60:64]))
        step = 2.0 * DENSE_W / (NDEN + 1)
        for j in range(NDEN):
            vred(lambda v, j=j, st=step: v.tensor_scalar_add(
                qtmp[0:1, 64 + j:65 + j], qrow[0:1, 61:62], float(st * (j + 1))))
        bcast_row(qcol[:, 0:NDEN], qtmp[0:1, 64:64 + NDEN], NDEN)
        bcast_row(qcol[:, 30:31], qrow[0:1, 63:64], 1)
        vop(lambda v: v.memset(qacc[:, 0:NDEN + 1], 0.0))
        vop(lambda v, W=W: v.memset(qbuf[:, 0:W], -1.0))
        for i in range(ntiles):
            datagen(i, t2[:, 0:TT])
            for j in range(NDEN):
                vop(lambda v, j=j: v.tensor_scalar(
                    qtmp[:, 0:TT], t2[:], qcol[:, j:j + 1], 0.0,
                    ALU.is_gt, ALU.add, accum_out=qacc[:, 32 + j:33 + j]))
            n = vop(lambda v: v.tensor_scalar(
                qtmp[:, 0:TT], t2[:], qcol[:, 30:31], 0.0,
                ALU.is_gt, ALU.add, accum_out=qacc[:, 32 + NDEN:33 + NDEN]))
            p.wait("vector", "VE", n)
            for j in range(NDEN + 1):
                vop(lambda v, j=j: v.tensor_add(qacc[:, j:j + 1], qacc[:, j:j + 1],
                                                qacc[:, 32 + j:33 + j]))
            vop(lambda v: v.tensor_scalar(qtmp[:, 0:TT], t2[:], qcol[:, 30:31], 0.0,
                                          ALU.is_le, ALU.add))
            vop(lambda v: v.tensor_mul(qtmp[:, TT:2 * TT], t2[:], qtmp[:, 0:TT]))
            vop(lambda v: v.tensor_scalar_add(qtmp[:, 0:TT], qtmp[:, 0:TT], -1.0))
            vop(lambda v: v.tensor_add(qtmp[:, TT:2 * TT], qtmp[:, TT:2 * TT],
                                       qtmp[:, 0:TT]))
            for r in range(NTOP // 8):
                n = vop(lambda v, i=i, r=r: v.max(
                    out=qbuf[:, i * NTOP + r * 8:i * NTOP + r * 8 + 8],
                    in_=qtmp[:, TT:2 * TT]))
                p.wait("vector", "VE", n)
                n = vop(lambda v, i=i, r=r: v.match_replace(
                    out=qtmp[:, TT:2 * TT],
                    in_to_replace=qbuf[:, i * NTOP + r * 8:i * NTOP + r * 8 + 8],
                    in_values=qtmp[:, TT:2 * TT], imm_value=-1.0))
                p.wait("vector", "VE", n)
        pfold(qrow[0:1, 0:NDEN + 1], qacc[:, 0:NDEN + 1], NDEN + 1, ALU.add)
        allreduce(NDEN + 1, ALU.add, qrow[0:1, 0:NDEN + 1], qrow[0:1, 0:NDEN + 1])
        vred(lambda v: v.memset(qrow[0:1, 100:101], float(Mv)))
        vred(lambda v: v.tensor_scalar_add(qrow[0:1, 100:101], qrow[0:1, 100:101], -1.0))
        vred(lambda v: v.tensor_scalar_mul(qrow[0:1, 101:102], qrow[0:1, 100:101], C995_HI))
        vred(lambda v: v.tensor_scalar_mul(qrow[0:1, 102:103], qrow[0:1, 100:101], C995_LO))
        vred(lambda v: v.tensor_add(qrow[0:1, 101:102], qrow[0:1, 101:102],
                                   qrow[0:1, 102:103]))
        vred(lambda v: v.tensor_copy(out=msk[0:1, 0:1], in_=qrow[0:1, 101:102]))
        vred(lambda v: v.tensor_copy(out=qrow[0:1, 103:104], in_=msk[0:1, 0:1]))
        vred(lambda v: v.tensor_tensor(qrow[0:1, 104:105], qrow[0:1, 103:104],
                                      qrow[0:1, 101:102], ALU.is_gt))
        vred(lambda v: v.tensor_sub(qrow[0:1, 103:104], qrow[0:1, 103:104],
                                   qrow[0:1, 104:105]))
        vred(lambda v: v.tensor_sub(qrow[0:1, 105:106], qrow[0:1, 101:102],
                                   qrow[0:1, 103:104]))
        vred(lambda v: v.memset(qrow[0:1, 107:108], 1.0))
        vred(lambda v: v.tensor_sub(qrow[0:1, 106:107], qrow[0:1, 107:108],
                                   qrow[0:1, 105:106]))
        vred(lambda v: v.tensor_single_scalar(qrow[0:1, 108:109], qrow[0:1, 105:106],
                                             0.0, ALU.is_gt))
        vred(lambda v: v.tensor_add(qrow[0:1, 108:109], qrow[0:1, 103:104],
                                   qrow[0:1, 108:109]))
        vred(lambda v: v.tensor_sub(qrow[0:1, 110:111], qrow[0:1, 100:101],
                                   qrow[0:1, 103:104]))
        vred(lambda v: v.tensor_sub(qrow[0:1, 111:112], qrow[0:1, 100:101],
                                   qrow[0:1, 108:109]))
        for j in range(NDEN):
            vred(lambda v, j=j, st=step: v.tensor_scalar_add(
                qtmp[0:1, 64 + j:65 + j], qrow[0:1, 61:62], float(st * (j + 1))))
        vred(lambda v: v.tensor_scalar(qtmp[0:1, 0:NDEN], qrow[0:1, 0:NDEN],
                                      qrow[0:1, 110:111], 0.0, ALU.is_gt, ALU.add))
        vred(lambda v: v.tensor_mul(qtmp[0:1, 32:32 + NDEN], qtmp[0:1, 0:NDEN],
                                   qtmp[0:1, 64:64 + NDEN]))
        vred(lambda v: v.tensor_reduce(qrow[0:1, 116:117], qtmp[0:1, 32:32 + NDEN],
                                       AXX, ALU.max))
        vred(lambda v: v.tensor_tensor(qrow[0:1, 112:113], qrow[0:1, 116:117],
                                      qrow[0:1, 61:62], ALU.max))
        vred(lambda v: v.tensor_scalar(qtmp[0:1, 0:NDEN], qrow[0:1, 0:NDEN],
                                      qrow[0:1, 111:112], 0.0, ALU.is_le, ALU.add))
        vred(lambda v: v.tensor_scalar(qtmp[0:1, 32:32 + NDEN], qtmp[0:1, 0:NDEN],
                                      -1.0, -1e6, ALU.add, ALU.mult))
        vred(lambda v: v.tensor_add(qtmp[0:1, 32:32 + NDEN], qtmp[0:1, 32:32 + NDEN],
                                   qtmp[0:1, 64:64 + NDEN]))
        vred(lambda v: v.tensor_reduce(qrow[0:1, 116:117], qtmp[0:1, 32:32 + NDEN],
                                       AXX, ALU.min))
        vred(lambda v: v.tensor_tensor(qrow[0:1, 113:114], qrow[0:1, 116:117],
                                      qrow[0:1, 63:64], ALU.min))
        vred(lambda v: v.tensor_copy(out=qrow[0:1, 114:115], in_=qrow[0:1, NDEN:NDEN + 1]))
        for rk in range(2):
            vred(lambda v, rk=rk: v.tensor_copy(
                out=qrow[0:1, 120 + 4 * rk:121 + 4 * rk], in_=qrow[0:1, 112:113]))
            vred(lambda v, rk=rk: v.tensor_copy(
                out=qrow[0:1, 121 + 4 * rk:122 + 4 * rk], in_=qrow[0:1, 113:114]))
        for rnd in range(NBIS):
            for rk in range(2):
                vred(lambda v, rk=rk: v.tensor_sub(
                    qrow[0:1, 122 + 4 * rk:123 + 4 * rk],
                    qrow[0:1, 121 + 4 * rk:122 + 4 * rk],
                    qrow[0:1, 120 + 4 * rk:121 + 4 * rk]))
                for j in range(TBIS):
                    fr = (j + 1) / (TBIS + 1.0)
                    vred(lambda v, rk=rk, j=j, fr=fr: v.tensor_scalar_mul(
                        qtmp[0:1, 64 + rk * TBIS + j:65 + rk * TBIS + j],
                        qrow[0:1, 122 + 4 * rk:123 + 4 * rk], float(fr)))
                    vred(lambda v, rk=rk, j=j: v.tensor_add(
                        qtmp[0:1, 64 + rk * TBIS + j:65 + rk * TBIS + j],
                        qtmp[0:1, 64 + rk * TBIS + j:65 + rk * TBIS + j],
                        qrow[0:1, 120 + 4 * rk:121 + 4 * rk]))
            bcast_row(qcol[:, 0:2 * TBIS], qtmp[0:1, 64:64 + 2 * TBIS], 2 * TBIS)
            vop(lambda v: v.memset(qacc[:, 0:2 * TBIS], 0.0))
            for j in range(2 * TBIS):
                vop(lambda v, j=j, W=W: v.tensor_scalar(
                    qtmp[:, 1024:1024 + W], qbuf[:, 0:W], qcol[:, j:j + 1],
                    qacc[:, j:j + 1], ALU.is_gt, ALU.add,
                    accum_out=qacc[:, j:j + 1]))
            pfold(qrow[0:1, 0:2 * TBIS], qacc[:, 0:2 * TBIS], 2 * TBIS, ALU.add)
            if rnd == 0:
                vred(lambda v: v.tensor_copy(out=dbgs[0:1, 56:58], in_=qrow[0:1, 0:2]))
                vred(lambda v: v.tensor_copy(out=dbgs[0:1, 58:60], in_=qtmp[0:1, 64:66]))
            allreduce(2 * TBIS, ALU.add, qrow[0:1, 0:2 * TBIS], qrow[0:1, 0:2 * TBIS])
            if rnd == 0:
                vred(lambda v: v.tensor_copy(out=dbgs[0:1, 60:62], in_=qrow[0:1, 0:2]))
            vred(lambda v: v.tensor_scalar(qrow[0:1, 0:2 * TBIS], qrow[0:1, 0:2 * TBIS],
                                          qrow[0:1, 114:115], 0.0, ALU.add, ALU.add))
            for rk in range(2):
                rr = 110 + rk
                vred(lambda v, rk=rk, rr=rr: v.tensor_scalar(
                    qtmp[0:1, 0:TBIS], qrow[0:1, rk * TBIS:(rk + 1) * TBIS],
                    qrow[0:1, rr:rr + 1], 0.0, ALU.is_gt, ALU.add))
                vred(lambda v, rk=rk: v.tensor_mul(
                    qtmp[0:1, 32:32 + TBIS], qtmp[0:1, 0:TBIS],
                    qtmp[0:1, 64 + rk * TBIS:64 + (rk + 1) * TBIS]))
                vred(lambda v: v.tensor_reduce(qrow[0:1, 116:117],
                                               qtmp[0:1, 32:32 + TBIS], AXX, ALU.max))
                vred(lambda v, rk=rk: v.tensor_tensor(
                    qrow[0:1, 120 + 4 * rk:121 + 4 * rk],
                    qrow[0:1, 120 + 4 * rk:121 + 4 * rk], qrow[0:1, 116:117], ALU.max))
                vred(lambda v, rk=rk, rr=rr: v.tensor_scalar(
                    qtmp[0:1, 0:TBIS], qrow[0:1, rk * TBIS:(rk + 1) * TBIS],
                    qrow[0:1, rr:rr + 1], 0.0, ALU.is_le, ALU.add))
                vred(lambda v, rk=rk: v.tensor_mul(
                    qtmp[0:1, 32:32 + TBIS], qtmp[0:1, 0:TBIS],
                    qtmp[0:1, 64 + rk * TBIS:64 + (rk + 1) * TBIS]))
                vred(lambda v, rk=rk: v.tensor_scalar(
                    qtmp[0:1, 0:TBIS], qtmp[0:1, 0:TBIS], -1.0, -1e6,
                    ALU.add, ALU.mult))
                vred(lambda v, rk=rk: v.tensor_add(
                    qtmp[0:1, 32:32 + TBIS], qtmp[0:1, 32:32 + TBIS], qtmp[0:1, 0:TBIS]))
                vred(lambda v: v.tensor_reduce(qrow[0:1, 116:117],
                                               qtmp[0:1, 32:32 + TBIS], AXX, ALU.min))
                vred(lambda v, rk=rk: v.tensor_tensor(
                    qrow[0:1, 121 + 4 * rk:122 + 4 * rk],
                    qrow[0:1, 121 + 4 * rk:122 + 4 * rk], qrow[0:1, 116:117], ALU.min))
        for rk in range(2):
            bcast_row(qcol[:, 40:41], qrow[0:1, 120 + 4 * rk:121 + 4 * rk], 1)
            bcast_row(qcol[:, 41:42], qrow[0:1, 121 + 4 * rk:122 + 4 * rk], 1)
            h = W // 2
            for part in range(2):
                lo_, hi_ = part * h, (part + 1) * h
                vop(lambda v, lo_=lo_, hi_=hi_: v.tensor_scalar(
                    qtmp[:, 0:h] if False else qtmp[:, lo_:hi_], qbuf[:, lo_:hi_],
                    qcol[:, 40:41], 0.0, ALU.is_gt, ALU.add))
                vop(lambda v, lo_=lo_, hi_=hi_: v.tensor_mul(
                    qtmp[:, lo_:hi_], qtmp[:, lo_:hi_], qbuf[:, lo_:hi_]))
                vop(lambda v, lo_=lo_, hi_=hi_: v.tensor_scalar(
                    qtmp[:, 2048 + lo_:2048 + hi_], qbuf[:, lo_:hi_],
                    qcol[:, 41:42], 0.0, ALU.is_le, ALU.add))
                vop(lambda v, lo_=lo_, hi_=hi_: v.tensor_mul(
                    qtmp[:, lo_:hi_], qtmp[:, lo_:hi_],
                    qtmp[:, 2048 + lo_:2048 + hi_]))
            vred(lambda v, W=W: v.tensor_reduce(qcol[:, 42:43], qtmp[:, 0:W],
                                                AXX, ALU.max))
            pfold(qrow[0:1, 117 + rk:118 + rk], qcol[:, 42:43], 1, ALU.max)
        allreduce(2, ALU.max, qrow[0:1, 117:119], qrow[0:1, 117:119])
        vred(lambda v: v.tensor_copy(out=dbgs[0:1, 40:41], in_=qrow[0:1, 60:61]))
        vred(lambda v: v.tensor_copy(out=dbgs[0:1, 41:42], in_=qrow[0:1, 61:62]))
        vred(lambda v: v.tensor_copy(out=dbgs[0:1, 42:43], in_=qrow[0:1, 63:64]))
        vred(lambda v: v.tensor_copy(out=dbgs[0:1, 43:44], in_=qrow[0:1, 101:102]))
        vred(lambda v: v.tensor_copy(out=dbgs[0:1, 44:45], in_=qrow[0:1, 103:104]))
        vred(lambda v: v.tensor_copy(out=dbgs[0:1, 45:46], in_=qrow[0:1, 110:111]))
        vred(lambda v: v.tensor_copy(out=dbgs[0:1, 46:47], in_=qrow[0:1, NDEN:NDEN + 1]))
        vred(lambda v: v.tensor_copy(out=dbgs[0:1, 47:48], in_=qrow[0:1, 112:113]))
        vred(lambda v: v.tensor_copy(out=dbgs[0:1, 48:49], in_=qrow[0:1, 113:114]))
        vred(lambda v: v.tensor_copy(out=dbgs[0:1, 49:50], in_=qrow[0:1, 120:121]))
        vred(lambda v: v.tensor_copy(out=dbgs[0:1, 50:51], in_=qrow[0:1, 121:122]))
        vred(lambda v: v.tensor_copy(out=dbgs[0:1, 51:52], in_=qrow[0:1, 117:118]))
        vred(lambda v: v.tensor_copy(out=dbgs[0:1, 52:53], in_=qrow[0:1, 118:119]))
        vred(lambda v: v.tensor_copy(out=dbgs[0:1, 53:54], in_=qrow[0:1, 0:1]))
        vred(lambda v: v.tensor_copy(out=dbgs[0:1, 54:55], in_=qrow[0:1, 105:106]))
        vred(lambda v: v.tensor_copy(out=dbgs[0:1, 55:56], in_=qrow[0:1, 106:107]))
        vred(lambda v: v.tensor_mul(qrow[0:1, 117:118], qrow[0:1, 117:118],
                                   qrow[0:1, 106:107]))
        vred(lambda v: v.tensor_mul(qrow[0:1, 118:119], qrow[0:1, 118:119],
                                   qrow[0:1, 105:106]))
        vred(lambda v, sb=sbase: v.tensor_add(qrow[0:1, sb:sb + 1], qrow[0:1, 117:118],
                                             qrow[0:1, 118:119]))
        vred(lambda v, sb=sbase: v.tensor_scalar_max(qrow[0:1, sb:sb + 1],
                                                    qrow[0:1, sb:sb + 1], 1e-5))
        vred(lambda v, sb=sbase: v.reciprocal(qrow[0:1, sb + 1:sb + 2],
                                              qrow[0:1, sb:sb + 1]))
        vred(lambda v, sb=sbase: v.tensor_scalar_mul(qrow[0:1, sb + 1:sb + 2],
                                                    qrow[0:1, sb + 1:sb + 2], 127.0))
        vred(lambda v, sb=sbase: v.reciprocal(qrow[0:1, sb + 2:sb + 3],
                                              qrow[0:1, sb + 1:sb + 2]))
        bcast_row(qcol[:, scol:scol + 2], qrow[0:1, sbase + 1:sbase + 3], 2)

    if limit < 4: return _finish(nc, p, e, ctx, outp, dbg, dbgs, t0, outp_dummy=True)
    for tq in range(NTQ):
        for k in range(NKC):
            base = (tq * NKC + k) * 8
            vop(lambda v, tq=tq, base=base: v.tensor_copy(
                out=qtmp[:, base:base + 8],
                in_=bc[:, 2 * BCW + 3 + tq * TT:2 * BCW + 3 + (tq + 1) * TT:64]))
            vop(lambda v, tq=tq, base=base: v.tensor_copy(
                out=qtmp[:, 1024 + base:1024 + base + 8],
                in_=bc[:, 3 * BCW + 3 + tq * TT:3 * BCW + 3 + (tq + 1) * TT:64]))
            vop(lambda v, base=base: v.tensor_sub(
                pil[:, base:base + 8], pil[:, base:base + 8], qtmp[:, base:base + 8]))
            vop(lambda v, base=base: v.tensor_mul(
                pil[:, base:base + 8], pil[:, base:base + 8],
                qtmp[:, 1024 + base:1024 + base + 8]))
            vop(lambda v, k=k, base=base: v.tensor_scalar(
                pil[:, base:base + 8], pil[:, base:base + 8],
                ccol("ffn_g", k), ccol("ffn_b", k), ALU.mult, ALU.add))
    v = p.cur("VE")
    p.wait("scalar", "VE", v)
    s = sop(lambda s: s.activation(out=pil[:, 0:256], in_=pil[:, 0:256], func=AF.Abs))
    p.wait("vector", "SE", s)

    def q1_datagen(i, dst):
        k, tq = i % NKC, i // NKC
        v = h1f_tile(k, tq, dst)
        p.wait("scalar", "VE", v)
        s = sop(lambda s, d=dst: s.activation(out=d, in_=d, func=AF.Abs))
        p.wait("vector", "SE", s)

    qpath(256, Q1_PLO, Q1_PHI, M1, q1_datagen, NKC * NTQ, sbase=150, scol=44)
    vred(lambda v: v.tensor_copy(out=dbgs[0:1, 0:3], in_=qrow[0:1, 150:153]))
    join("vector", "scalar", "gpsimd", "tensor")

    if limit < 5: return _finish(nc, p, e, ctx, outp, dbg, dbgs, t0, outp_dummy=True)
    hc = arena
    for tq in range(NTQ):
        for k in range(NKC):
            lo = tq * TT
            p.wait("gpsimd", "VE", p.cur("VE"))
            g = gld(t3[:, 0:TT + 3], xn_sp[k * 128:(k + 1) * 128, lo:lo + TT + 3])
            p.wait("vector", "GD", g)
            vop(lambda v, lo=lo: v.tensor_sub(t3[:, 0:TT + 3], t3[:, 0:TT + 3],
                                              bcs(2, lo, TT + 3)))
            vop(lambda v, lo=lo: v.tensor_mul(t3[:, 0:TT + 3], t3[:, 0:TT + 3],
                                              bcs(3, lo, TT + 3)))
            vop(lambda v, k=k: v.tensor_scalar(
                t3[:, 0:TT + 3], t3[:, 0:TT + 3],
                ccol("ln1_g", k), ccol("ln1_b", k), ALU.mult, ALU.add))
            if tq == 0:
                vop(lambda v: v.tensor_single_scalar(out=t3[:, 0:3], in_=t3[:, 0:3], scalar=pcb[:, 0:1], op=ALU.mult))
            dst = hc[:, k * TPC + tq * TT:k * TPC + (tq + 1) * TT]
            vop(lambda v, k=k, d=dst: v.tensor_scalar(
                d, t3[:, 0:TT], dwb[:, k * 4:k * 4 + 1], 0.0, ALU.mult, ALU.add))
            for tap in range(1, 4):
                vop(lambda v, k=k, tap=tap, d=dst: v.scalar_tensor_tensor(
                    d, t3[:, tap:tap + TT], dwb[:, k * 4 + tap:k * 4 + tap + 1], d,
                    ALU.mult, ALU.add))
            vop(lambda v, k=k, d=dst: v.tensor_scalar(
                d, d, ccol("dw_b", k), 0.0, ALU.add, ALU.add))
    p.wait("tensor", "VE", p.cur("VE"))
    p.wait("scalar", "VE", p.cur("VE"))
    g = gld(qtmp[:, 0:128], xh)
    p.wait("vector", "GD", g)
    p.wait("scalar", "GD", g)
    vred(lambda v: v.memset(t0[:], 0.0))
    for k in range(NKC):
        vred(lambda v, k=k: v.tensor_copy(out=t0[:, 0:3], in_=qtmp[:, k * 3:k * 3 + 3]))
        if k == 0:
            vred(lambda v: v.tensor_copy(out=dbgs[0:1, 36:39], in_=t0[0:1, 0:3]))
        vred(lambda v, k=k: v.tensor_copy(out=h1h[:, k * 3:k * 3 + 3], in_=t0[:, 0:3]))
        s_ = sop(lambda sc: sc.square(out=t1[:], in_=t0[:]))
        p.wait("tensor", "SE", s_)
        top(lambda tn, k=k: tn.matmul(pb[6][0:1, :], onesb[:, 0:1], t0[:],
                                      start=(k == 0), stop=(k == NKC - 1)))
        t = top(lambda tn, k=k: tn.matmul(pb[7][0:1, :], onesb[:, 0:1], t1[:],
                                          start=(k == 0), stop=(k == NKC - 1)))
        p.wait("vector", "TE", t)
        p.wait("scalar", "TE", t)
    def _halo_rowstats(dst_mu, dst_rs):
        vred(lambda v: v.tensor_copy(out=qrow[0:1, 200:203], in_=pb[6][0:1, 0:3]))
        vred(lambda v: v.tensor_copy(out=qrow[0:1, 204:207], in_=pb[7][0:1, 0:3]))
        vred(lambda v: v.tensor_scalar_mul(qrow[0:1, 200:203], qrow[0:1, 200:203], 1.0 / D))
        vred(lambda v: v.tensor_scalar_mul(qrow[0:1, 204:207], qrow[0:1, 204:207], 1.0 / D))
        vred(lambda v: v.tensor_tensor(qrow[0:1, 208:211], qrow[0:1, 200:203],
                                       qrow[0:1, 200:203], ALU.mult))
        vred(lambda v: v.tensor_sub(qrow[0:1, 204:207], qrow[0:1, 204:207],
                                    qrow[0:1, 208:211]))
        vv = vred(lambda v: v.tensor_scalar_add(qrow[0:1, 204:207], qrow[0:1, 204:207], 1e-5))
        p.wait("scalar", "VE", vv)
        s_ = sop(lambda sc: sc.activation(out=qrow[0:1, 204:207], in_=qrow[0:1, 204:207],
                                          func=AF.Sqrt))
        p.wait("vector", "SE", s_)
        vred(lambda v: v.reciprocal(qrow[0:1, 204:207], qrow[0:1, 204:207]))
        bcast_row(dst_mu, qrow[0:1, 200:203], 3)
        bcast_row(dst_rs, qrow[0:1, 204:207], 3)
    _halo_rowstats(qcol[:, 56:59], qcol[:, 59:62])
    vred(lambda v: v.tensor_copy(out=dbgs[0:1, 62:63], in_=qcol[0:1, 56:57]))
    vred(lambda v: v.tensor_copy(out=dbgs[0:1, 63:64], in_=qcol[0:1, 59:60]))
    p.wait("tensor", "VE", p.cur("VE"))
    vred(lambda v: v.memset(t0[:], 0.0))
    for k in range(NKC):
        vop(lambda v, k=k: v.tensor_sub(t0[:, 0:3], h1h[:, k * 3:k * 3 + 3], qcol[:, 56:59]))
        v = vop(lambda v: v.tensor_mul(t0[:, 0:3], t0[:, 0:3], qcol[:, 59:62]))
        vred(lambda v, k=k: v.tensor_copy(out=h1h[:, k * 3:k * 3 + 3], in_=t0[:, 0:3]))
        p.wait("scalar", "VE", p.cur("VE"))
        s_ = sop(lambda sc: sc.square(out=t1[:], in_=t0[:]))
        p.wait("tensor", "SE", s_)
        top(lambda tn, k=k: tn.matmul(pb[6][0:1, :], onesb[:, 0:1], t0[:],
                                      start=(k == 0), stop=(k == NKC - 1)))
        t = top(lambda tn, k=k: tn.matmul(pb[7][0:1, :], onesb[:, 0:1], t1[:],
                                          start=(k == 0), stop=(k == NKC - 1)))
        p.wait("vector", "TE", t)
        p.wait("scalar", "TE", t)
    _halo_rowstats(qcol[:, 56:59], qcol[:, 59:62])
    p.wait("vector", "TE", p.cur("TE"))
    for k in range(NKC):
        vop(lambda v, k=k: v.tensor_sub(h1h[:, k * 3:k * 3 + 3], h1h[:, k * 3:k * 3 + 3],
                                        qcol[:, 56:59]))
        vop(lambda v, k=k: v.tensor_mul(h1h[:, k * 3:k * 3 + 3], h1h[:, k * 3:k * 3 + 3],
                                        qcol[:, 59:62]))
        vop(lambda v, k=k: v.tensor_scalar(h1h[:, k * 3:k * 3 + 3], h1h[:, k * 3:k * 3 + 3],
                                           ccol("ln1_g", k), ccol("ln1_b", k),
                                           ALU.mult, ALU.add))
        vred(lambda v, k=k: v.tensor_single_scalar(
            out=h1h[:, k * 3:k * 3 + 3], in_=h1h[:, k * 3:k * 3 + 3],
            scalar=pcb[:, 0:1], op=ALU.mult))
    join("vector", "tensor", "sync", "scalar", "gpsimd")

    te_slab = [0, 0]
    for o in range(NKC):
        half = o % 2
        if o >= 2:
            p.wait("sync", "TE", te_slab[half])
        l = ld_tiles(half * 2048, pwT[0:NKC, o], 8)
        p.wait("tensor", "LD", l)
        for tq in range(NTQ):
            pbk = pb[(o * NTQ + tq) % 4]
            for k in range(NKC):
                top(lambda tn, k=k, half=half, pbk=pbk, tq=tq: tn.matmul(
                    pbk[:], wsl[:, half * 2048 + k * 128:half * 2048 + (k + 1) * 128],
                    hc[:, k * TPC + tq * TT:k * TPC + (tq + 1) * TT],
                    start=(k == 0), stop=(k == NKC - 1)))
            t = p.cur("TE")
            p.wait("scalar", "TE", t)
            s = sop(lambda sc, o=o, pbk=pbk: sc.activation(
                out=t0[:], in_=pbk[:], func=AF.Identity, bias=ccol("pw_b", o)))
            p.wait("vector", "SE", s)
            p.wait("gpsimd", "SE", s)
            g = gld(t1[:], xn_sp[o * 128:(o + 1) * 128, 3 + tq * TT:3 + (tq + 1) * TT])
            p.wait("vector", "GD", g)
            v = vop(lambda v: v.tensor_add(t0[:], t0[:], t1[:]))
            if o == 0 and tq == 0:
                vred(lambda v: v.tensor_copy(out=dbgs[0:1, 52:54], in_=t0[0:1, 0:2]))
            p.wait("gpsimd", "VE", v)
            g = gst(x1_sp[o * 128:(o + 1) * 128, tq * TT:(tq + 1) * TT], t0[:])
            p.wait("scalar", "GD", g)
            p.wait("tensor", "SE", p.cur("SE"))
        te_slab[half] = p.cur("TE")
    join("vector", "scalar", "gpsimd", "tensor", "sync")

    def s3_src(k, tq):
        p.wait("gpsimd", "VE", p.cur("VE"))
        g = gld(t0[:], x1_sp[k * 128:(k + 1) * 128, tq * TT:(tq + 1) * TT])
        p.wait("vector", "GD", g)
        p.wait("scalar", "GD", g)
        vop(lambda v: v.engine_nop())

    stats_pass(s3_src, 0, 1)
    join("vector", "scalar", "gpsimd", "tensor")

    if limit < 6: return _finish(nc, p, e, ctx, outp, dbg, dbgs, t0, outp_dummy=True)
    xq = arena
    for tq in range(NTQ):
        for k in range(NKC):
            dst = xq[:, k * TPC + tq * TT:k * TPC + (tq + 1) * TT]
            h1f_tile(k, tq, t0[:, 0:TT])
            vop(lambda v, d=dst: v.tensor_single_scalar(out=d, in_=t0[:], scalar=qcol[:, 44:45], op=ALU.mult))
            vop(lambda v, d=dst: v.tensor_copy(out=msk[:], in_=d))
            vop(lambda v, d=dst: v.tensor_copy(out=d, in_=msk[:]))
            vop(lambda v, d=dst: v.tensor_scalar_max(d, d, -128.0))
            vop(lambda v, d=dst: v.tensor_scalar_min(d, d, 127.0))
            vop(lambda v, d=dst: v.tensor_single_scalar(out=d, in_=d, scalar=qcol[:, 45:46], op=ALU.mult))
    join("vector", "tensor", "gpsimd", "scalar", "sync")

    te_slab = [0, 0]
    se_bank4 = [0, 0, 0, 0]
    gd_par4 = [0, 0]
    ve_par4 = [0, 0]
    for o in range(NHC):
        half = o % 2
        if o >= 2:
            p.wait("sync", "TE", te_slab[half])
        l = ld_tiles(half * 2048, f1T[0:NKC, o], 8)
        p.wait("gpsimd", "LD", l)
        d = wsl[:, half * 2048:half * 2048 + 8 * 128]
        gop(lambda g, d=d: g.tensor_single_scalar(out=d, in_=d, scalar=qcol[:, 52:53], op=ALU.mult))
        gop(lambda g, d=d: g.tensor_copy(out=wint[:, 0:1024], in_=d))
        gop(lambda g, d=d: g.tensor_copy(out=d, in_=wint[:, 0:1024]))
        gop(lambda g, d=d: g.tensor_scalar_max(d, d, -1.0))
        gop(lambda g, d=d: g.tensor_scalar_min(d, d, 1.0))
        gq = gop(lambda g, d=d: g.tensor_single_scalar(out=d, in_=d, scalar=qcol[:, 50:51], op=ALU.mult))
        p.wait("tensor", "GE", gq)
        for tq in range(NTQ):
            step = o * NTQ + tq
            bk = step % 4
            par = step % 2
            et = t0[:] if par == 0 else t3[:, 0:TT]
            pbk = pb[bk]
            p.wait("tensor", "SE", se_bank4[bk])
            for k in range(NKC):
                top(lambda tn, k=k, half=half, pbk=pbk, tq=tq: tn.matmul(
                    pbk[:], wsl[:, half * 2048 + k * 128:half * 2048 + (k + 1) * 128],
                    xq[:, k * TPC + tq * TT:k * TPC + (tq + 1) * TT],
                    start=(k == 0), stop=(k == NKC - 1)))
            t = p.cur("TE")
            p.wait("scalar", "TE", t)
            p.wait("scalar", "GD", gd_par4[par])
            p.wait("scalar", "VE", ve_par4[par])
            s = sop(lambda sc, pbk=pbk, et=et: sc.activation(out=et, in_=pbk[:],
                                                            func=AF.Gelu))
            se_bank4[bk] = s
            p.wait("gpsimd", "SE", s)
            p.wait("vector", "SE", s)
            gst(ge_sp[o, :, tq * TT:(tq + 1) * TT], et)
            gd_par4[par] = p.cur("GD")
            vop(lambda v, o=o, tq=tq, et=et: v.tensor_copy(
                out=pil[:, (o * NTQ + tq) * 8:(o * NTQ + tq) * 8 + 8],
                in_=et[:, 0:TT:64]))
            ve_par4[par] = p.cur("VE")
        te_slab[half] = p.cur("TE")
    join("vector", "scalar", "gpsimd", "tensor", "sync")
    s = sop(lambda s: s.activation(out=pil[:, 0:1024], in_=pil[:, 0:1024], func=AF.Abs))
    p.wait("vector", "SE", s)

    if limit < 7: return _finish(nc, p, e, ctx, outp, dbg, dbgs, t0, outp_dummy=True)
    ga = arena
    for tq in range(NTQ):
        for k in range(NKC):
            p.wait("gpsimd", "VE", p.cur("VE"))
            g = gld(t3[:, 0:TT], x1_sp[k * 128:(k + 1) * 128, tq * TT:(tq + 1) * TT])
            p.wait("vector", "GD", g)
            dst = ga[:, k * TPC + tq * TT:k * TPC + (tq + 1) * TT]
            vop(lambda v, tq=tq, d=dst: v.tensor_sub(d, t3[:, 0:TT],
                                                     bcs(0, 3 + tq * TT, TT)))
            vop(lambda v, tq=tq, d=dst: v.tensor_mul(d, d, bcs(1, 3 + tq * TT, TT)))
            vop(lambda v, k=k, d=dst: v.tensor_scalar(
                d, d, ccol("ln2_g", k), ccol("ln2_b", k), ALU.mult, ALU.add))
    vop(lambda v: v.memset(stc[:, 0:12], 0.0))
    vop(lambda v: v.memset(stc[:, 12:24], 1.0))
    join("vector", "tensor", "scalar", "gpsimd", "sync")

    te_slab = [0, 0]
    for i in range(NEC):
        half = i % 2
        if i >= 2:
            p.wait("sync", "TE", te_slab[half])
        l = ld_tiles(half * 2048, fwT[0:NKC, i], 8)
        l = ld_tiles(half * 2048 + 8 * 128, fwT[0:NKC, NEC + i], 8)
        p.wait("tensor", "LD", l)
        for tq in range(NTQ):
            pbA = pb[4 + half * 2]
            pbB = pb[5 + half * 2]
            p.wait("tensor", "VE", p.cur("VE"))
            p.wait("tensor", "SE", p.cur("SE"))
            for k in range(NKC):
                top(lambda tn, k=k, half=half, pbA=pbA, tq=tq: tn.matmul(
                    pbA[:], wsl[:, half * 2048 + k * 128:half * 2048 + (k + 1) * 128],
                    ga[:, k * TPC + tq * TT:k * TPC + (tq + 1) * TT],
                    start=(k == 0), stop=(k == NKC - 1)))
            for k in range(NKC):
                top(lambda tn, k=k, half=half, pbB=pbB, tq=tq: tn.matmul(
                    pbB[:], wsl[:, half * 2048 + (8 + k) * 128:half * 2048 + (9 + k) * 128],
                    ga[:, k * TPC + tq * TT:k * TPC + (tq + 1) * TT],
                    start=(k == 0), stop=(k == NKC - 1)))
            t = p.cur("TE")
            p.wait("scalar", "TE", t)
            ct = cvb[:, half * 2 * TT:half * 2 * TT + TT]
            vt = cvb[:, half * 2 * TT + TT:half * 2 * TT + 2 * TT]
            p.wait("scalar", "VE", p.cur("VE"))
            p.wait("scalar", "GD", p.cur("GD"))
            sop(lambda sc, i=i, pbA=pbA, ct=ct: sc.activation(
                out=ct, in_=pbA[:], func=AF.Sigmoid, scale=-1.0,
                bias=fbb[:, 24 + i:25 + i]))
            sop(lambda sc, i=i, pbA=pbA: sc.activation(
                out=t0[:], in_=pbA[:], func=AF.Sigmoid, bias=fbb[:, i:i + 1]))
            sop(lambda sc, i=i, pbB=pbB: sc.activation(
                out=t1[:], in_=pbB[:], func=AF.Identity, bias=fbb[:, 12 + i:13 + i]))
            s = sop(lambda sc, i=i, pbB=pbB: sc.activation(
                out=t2[:], in_=pbB[:], func=AF.Sigmoid, bias=fbb[:, 12 + i:13 + i]))
            p.wait("vector", "SE", s)
            vop(lambda v: v.tensor_single_scalar(msk[:], t1[:], 0.0, ALU.is_ge))
            if i == 0 and tq == 3:
                vred(lambda v: v.tensor_copy(out=dbgs[0:1, 30:32], in_=t1[0:1, 510:512]))
                vred(lambda v: v.tensor_copy(out=dbgs[0:1, 32:33], in_=msk[0:1, 511:512]))
            vop(lambda v: v.tensor_scalar_add(t1[:], t1[:], 0.5))
            vop(lambda v: v.select(out=t2[:], mask=msk[:], on_true=t1[:], on_false=t2[:]))
            v = vop(lambda v, vt=vt: v.tensor_mul(vt, t0[:], t2[:]))
            p.wait("gpsimd", "VE", v)
            gst(cv_sp[i, 0, :, tq * TT:(tq + 1) * TT], ct)
            gst(cv_sp[i, 1, :, tq * TT:(tq + 1) * TT], vt)
            if i == 0 and tq == 0:
                vred(lambda v, ct=ct: v.tensor_copy(out=dbgs[0:1, 60:62], in_=ct[0:1, 0:2]))
                vred(lambda v, vt=vt: v.tensor_copy(out=dbgs[0:1, 62:64], in_=vt[0:1, 0:2]))
            if i == 0 and tq == 3:
                vred(lambda v, ct=ct: v.tensor_copy(out=dbgs[0:1, 36:38], in_=ct[0:1, 510:512]))
                vred(lambda v, vt=vt: v.tensor_copy(out=dbgs[0:1, 38:40], in_=vt[0:1, 510:512]))
            vred(lambda v, ct=ct, vt=vt, i=i: v.tensor_tensor_scan(
                out=qtmp[:, 0:TT], data0=ct, data1=vt, initial=stc[:, i:i + 1],
                op0=ALU.mult, op1=ALU.add))
            vred(lambda v, i=i: v.tensor_copy(out=stc[:, i:i + 1],
                                             in_=qtmp[:, TT - 1:TT]))
            if i == 0 and tq == 1:
                vred(lambda v: v.tensor_copy(out=dbgs[0:1, 33:34], in_=stc[0:1, 0:1]))
            vred(lambda v, ct=ct, i=i: v.tensor_tensor_scan(
                out=qtmp[:, 0:TT], data0=ct, data1=zer[:], initial=stc[:, 12 + i:13 + i],
                op0=ALU.mult, op1=ALU.add))
            vred(lambda v, i=i: v.tensor_copy(out=stc[:, 12 + i:13 + i],
                                             in_=qtmp[:, TT - 1:TT]))
        te_slab[half] = p.cur("TE")
    if limit < 7.3:
        return _finish(nc, p, e, ctx, outp, dbg, dbgs, t0)
    for i in range(NEC):
        vop(lambda v, i=i: v.scalar_tensor_tensor(
            qcol[:, i:i + 1], stc[:, 12 + i:13 + i], pcb[:, 1:2], stc[:, i:i + 1],
            ALU.mult, ALU.add))
    vred(lambda v: v.tensor_copy(out=dbgs[0:1, 56:57], in_=stc[0:1, 0:1]))
    vred(lambda v: v.tensor_copy(out=dbgs[0:1, 57:58], in_=stc[0:1, 12:13]))
    vred(lambda v: v.tensor_copy(out=dbgs[0:1, 58:59], in_=qcol[0:1, 0:1]))
    v = p.cur("VE")
    p.wait("tensor", "VE", v)
    t = top(lambda tn: tn.transpose(pb[7][0:12, 0:128], qcol[:, 0:12], identb[:]))
    p.wait("vector", "TE", t)
    v = vred(lambda v: v.tensor_copy(out=tr12[:], in_=pb[7][0:12, 0:128]))
    p.wait("gpsimd", "VE", v)
    g = gst(ag_i.rearrange("o (i f) -> (o i) f", i=12), tr12[:])
    p.wait("gpsimd", "GD", g)
    if limit < 7.4:
        return _finish(nc, p, e, ctx, outp, dbg, dbgs, t0)
    g = gld(rowsB[0:1, 0:EXP], ag_i)
    p.wait("vector", "GD", g)
    vop(lambda v: v.memset(rowsA[0:1, 0:EXP], 0.0))
    for pr in range(4):
        v = vop(lambda v, pr=pr: v.tensor_single_scalar(
            out=qtmp[0:1, 0:EXP], in_=rowsB[0:1, 0:EXP],
            scalar=pgmb[0:1, pr:pr + 1], op=ALU.mult))
        p.wait("gpsimd", "VE", v)
        g = gst(ar_i[0:1, 0:EXP], qtmp[0:1, 0:EXP])
        p.wait("gpsimd", "GD", g)
        c = p.inc("gpsimd", lambda s: s.collective_compute(
            "AllReduce", ALU.add, replica_groups=[list(range(NCORE))],
            ins=[ar_i[0:1, 0:EXP]], outs=[ar_o[0:1, 0:EXP]]), "CC", 1)
        p.wait("gpsimd", "CC", c)
        g = gld(qtmp[0:1, 0:EXP], ar_o[0:1, 0:EXP])
        p.wait("vector", "GD", g)
        vop(lambda v, pr=pr: v.tensor_single_scalar(
            out=qtmp[0:1, 0:EXP], in_=qtmp[0:1, 0:EXP],
            scalar=pgmb[0:1, 8 + pr:9 + pr], op=ALU.mult))
        vop(lambda v: v.tensor_add(rowsA[0:1, 0:EXP], rowsA[0:1, 0:EXP],
                                   qtmp[0:1, 0:EXP]))
    v = p.cur("VE")
    p.wait("gpsimd", "VE", v)
    g = gst(ag_i, rowsA[0:1, 0:EXP])
    p.wait("gpsimd", "GD", g)
    g = gld(tr12[:], ag_i.rearrange("o (i f) -> (o i) f", i=12))
    p.wait("tensor", "GD", g)
    p.wait("tensor", "VE", p.cur("VE"))
    t = top(lambda tn: tn.transpose(pb[6][:, 0:12], tr12[:], identb[0:12, 0:12]))
    p.wait("vector", "TE", t)
    vred(lambda v: v.tensor_copy(out=stc[:, 16:28], in_=pb[6][:, 0:12]))
    vv = vred(lambda v: v.tensor_scalar(stc[:, 16:28], stc[:, 16:28], pcb[:, 1:2], 0.0,
                                        ALU.add, ALU.add))
    p.wait("tensor", "VE", vv)
    vred(lambda v: v.tensor_copy(out=dbgs[0:1, 8:20], in_=stc[0:1, 16:28]))
    join("vector", "scalar", "gpsimd", "tensor", "sync")

    if limit < 8: return _finish(nc, p, e, ctx, outp, dbg, dbgs, t0, outp_dummy=True)
    for tq in range(NTQ):
        for i in range(NEC):
            half = i % 2
            p.wait("gpsimd", "VE", p.cur("VE"))
            gld(cvb[:, half * 2 * TT:half * 2 * TT + TT],
                cv_sp[i, 0, :, tq * TT:(tq + 1) * TT])
            g = gld(cvb[:, half * 2 * TT + TT:half * 2 * TT + 2 * TT],
                    cv_sp[i, 1, :, tq * TT:(tq + 1) * TT])
            p.wait("vector", "GD", g)
            vred(lambda v, i=i, half=half: v.tensor_tensor_scan(
                out=arena[:, i * TT:(i + 1) * TT],
                data0=cvb[:, half * 2 * TT:half * 2 * TT + TT],
                data1=cvb[:, half * 2 * TT + TT:half * 2 * TT + 2 * TT],
                initial=stc[:, 16 + i:17 + i], op0=ALU.mult, op1=ALU.add))
            vred(lambda v, i=i: v.tensor_copy(out=stc[:, 16 + i:17 + i],
                                             in_=arena[:, (i + 1) * TT - 1:(i + 1) * TT]))
        hsv = p.cur("VE")
        p.wait("tensor", "VE", hsv)
        p.wait("sync", "VE", hsv)
        for o in range(NKC):
            half = o % 2
            l = ld_tiles(half * 2048, dpT[0:NEC, o], 12)
            p.wait("tensor", "LD", l)
            pbk = pb[4 + (o % 4)]
            p.wait("tensor", "VE", p.cur("VE"))
            for i in range(NEC):
                top(lambda tn, i=i, half=half, pbk=pbk: tn.matmul(
                    pbk[:], wsl[:, half * 2048 + i * 128:half * 2048 + (i + 1) * 128],
                    arena[:, i * TT:(i + 1) * TT],
                    start=(i == 0), stop=(i == NEC - 1)))
            t = p.cur("TE")
            p.wait("vector", "TE", t)
            p.wait("vector", "GD", p.cur("GD"))
            g = gld(t1[:], x1_sp[o * 128:(o + 1) * 128, tq * TT:(tq + 1) * TT])
            p.wait("vector", "GD", g)
            v = vop(lambda v, pbk=pbk: v.tensor_add(t0[:], pbk[:], t1[:]))
            p.wait("gpsimd", "VE", v)
            g = gst(x2_sp[o * 128:(o + 1) * 128, tq * TT:(tq + 1) * TT], t0[:])
            p.wait("vector", "GD", g)
            p.wait("sync", "TE", t)
    join("vector", "scalar", "gpsimd", "tensor", "sync")

    def s4_src(k, tq):
        p.wait("gpsimd", "VE", p.cur("VE"))
        g = gld(t0[:], x2_sp[k * 128:(k + 1) * 128, tq * TT:(tq + 1) * TT])
        p.wait("vector", "GD", g)
        p.wait("scalar", "GD", g)
        vop(lambda v: v.engine_nop())

    stats_pass(s4_src, 2, 3)
    join("vector", "scalar", "gpsimd", "tensor")

    if limit < 9: return _finish(nc, p, e, ctx, outp, dbg, dbgs, t0, outp_dummy=True)
    ma = arena
    for tq in range(NTQ):
        for k in range(NKC):
            p.wait("gpsimd", "VE", p.cur("VE"))
            g = gld(t3[:, 0:TT], x2_sp[k * 128:(k + 1) * 128, tq * TT:(tq + 1) * TT])
            p.wait("vector", "GD", g)
            dst = ma[:, k * TPC + tq * TT:k * TPC + (tq + 1) * TT]
            vop(lambda v, tq=tq, d=dst: v.tensor_sub(d, t3[:, 0:TT],
                                                     bcs(2, 3 + tq * TT, TT)))
            vop(lambda v, tq=tq, d=dst: v.tensor_mul(d, d, bcs(3, 3 + tq * TT, TT)))
            vop(lambda v, k=k, d=dst: v.tensor_scalar(
                d, d, ccol("ln3_g", k), ccol("ln3_b", k), ALU.mult, ALU.add))
    join("vector", "tensor", "scalar", "gpsimd", "sync")

    te_slab = [0, 0]
    se_bank = [0, 0, 0, 0]
    gd_par = [0, 0]
    for h in range(NHC):
        half = h % 2
        if h >= 2:
            p.wait("sync", "TE", te_slab[half])
        l = ld_tiles(half * 2048, m1T[0:NKC, h], 8)
        p.wait("tensor", "LD", l)
        for tq in range(NTQ):
            step = h * NTQ + tq
            bk = step % 4
            par = step % 2
            et = t0[:] if par == 0 else t3[:, 0:TT]
            pbk = pb[bk]
            p.wait("tensor", "SE", se_bank[bk])
            for k in range(NKC):
                top(lambda tn, k=k, half=half, pbk=pbk, tq=tq: tn.matmul(
                    pbk[:], wsl[:, half * 2048 + k * 128:half * 2048 + (k + 1) * 128],
                    ma[:, k * TPC + tq * TT:k * TPC + (tq + 1) * TT],
                    start=(k == 0), stop=(k == NKC - 1)))
            t = p.cur("TE")
            p.wait("scalar", "TE", t)
            p.wait("scalar", "GD", gd_par[par])
            s = sop(lambda sc, h=h, pbk=pbk, et=et: sc.activation(
                out=et, in_=pbk[:], func=AF.Gelu, bias=mb1b[:, h:h + 1]))
            se_bank[bk] = s
            p.wait("gpsimd", "SE", s)
            gst(m2_sp[h, :, tq * TT:(tq + 1) * TT], et)
            gd_par[par] = p.cur("GD")
        te_slab[half] = p.cur("TE")
    join("vector", "scalar", "gpsimd", "tensor", "sync")

    if limit < 10: return _finish(nc, p, e, ctx, outp, dbg, dbgs, t0, outp_dummy=True)
    def q2_datagen(i, dst):
        h, tq = i % NHC, i // NHC
        p.wait("gpsimd", "SE", p.cur("SE"))
        g = gld(dst, ge_sp[h, :, tq * TT:(tq + 1) * TT])
        p.wait("scalar", "GD", g)
        s = sop(lambda s, d=dst: s.activation(out=d, in_=d, func=AF.Abs))
        p.wait("vector", "SE", s)

    qpath(1024, Q2_PLO, Q2_PHI, M2, q2_datagen, NHC * NTQ, sbase=160, scol=46)
    vred(lambda v: v.tensor_copy(out=dbgs[0:1, 3:6], in_=qrow[0:1, 160:163]))
    join("vector", "scalar", "gpsimd")

    for h in range(NHC):
        for tq in range(NTQ):
            p.wait("gpsimd", "VE", p.cur("VE"))
            g = gld(t0[:], ge_sp[h, :, tq * TT:(tq + 1) * TT])
            p.wait("vector", "GD", g)
            vop(lambda v: v.tensor_single_scalar(out=t1[:], in_=t0[:], scalar=qcol[:, 46:47], op=ALU.mult))
            vop(lambda v: v.tensor_copy(out=msk[:], in_=t1[:]))
            vop(lambda v: v.tensor_copy(out=t1[:], in_=msk[:]))
            vop(lambda v: v.tensor_scalar_max(t1[:], t1[:], -128.0))
            v = vop(lambda v: v.tensor_scalar_min(t1[:], t1[:], 127.0))
            if h == 0 and tq == 0:
                vred(lambda v: v.tensor_copy(out=dbgs[0:1, 28:30], in_=t1[0:1, 0:2]))
            p.wait("gpsimd", "VE", v)
            g = gst(q_sp[h, :, tq * TT:(tq + 1) * TT], t1[:])
            p.wait("vector", "GD", g)
    join("vector", "gpsimd")

    if limit < 11: return _finish(nc, p, e, ctx, outp, dbg, dbgs, t0, outp_dummy=True)
    for tq in range(NTQ):
        p.wait("tensor", "VE", p.cur("VE"))
        p.wait("tensor", "SE", p.cur("SE"))
        te_mark = [0] * NHC
        for h in range(NHC):
            half = h % 2
            if h >= 2:
                p.wait("gpsimd", "TE", te_mark[h - 2])
                p.wait("sync", "TE", te_mark[h - 2])
            g = gld(cvb[:, half * TT:half * TT + TT], m2_sp[h, :, tq * TT:(tq + 1) * TT])
            l = ld_tiles(half * 2048, m2T[h, 0:NKC], 8)
            p.wait("tensor", "LD", l)
            p.wait("tensor", "GD", g)
            for o in range(NKC):
                top(lambda tn, o=o, h=h, half=half: tn.matmul(
                    pb[o][:], wsl[:, half * 2048 + o * 128:half * 2048 + (o + 1) * 128],
                    cvb[:, half * TT:half * TT + TT],
                    start=(h == 0), stop=(h == NHC - 1)))
            te_mark[h] = p.cur("TE")
        t = p.cur("TE")
        p.wait("scalar", "TE", t)
        for o in range(NKC):
            p.wait("scalar", "GD", p.cur("GD"))
            s = sop(lambda sc, o=o: sc.activation(
                out=t0[:], in_=pb[o][:], func=AF.Identity, bias=ccol("mb2", o)))
            p.wait("vector", "SE", s)
            g = gld(t1[:], x2_sp[o * 128:(o + 1) * 128, tq * TT:(tq + 1) * TT])
            p.wait("vector", "GD", g)
            v = vop(lambda v: v.tensor_add(t0[:], t0[:], t1[:]))
            if o == 0 and tq == 0:
                vred(lambda v: v.tensor_copy(out=dbgs[0:1, 36:38], in_=t0[0:1, 0:2]))
            p.wait("gpsimd", "VE", v)
            gst(ao_sp[o * 128:(o + 1) * 128, tq * TT:(tq + 1) * TT], t0[:])
        p.wait("tensor", "SE", p.cur("SE"))
        p.wait("tensor", "GD", p.cur("GD"))
    join("vector", "scalar", "gpsimd", "tensor", "sync")

    if limit < 12: return _finish(nc, p, e, ctx, outp, dbg, dbgs, t0, outp_dummy=True)
    for tq in range(NTQ):
        p.wait("tensor", "VE", p.cur("VE"))
        te_mark = [0] * NHC
        for h in range(NHC):
            half = h % 2
            if h >= 2:
                p.wait("gpsimd", "TE", te_mark[h - 2])
                p.wait("sync", "TE", te_mark[h - 2])
            g = gld(cvb[:, half * TT:half * TT + TT], q_sp[h, :, tq * TT:(tq + 1) * TT])
            l = ld_tiles(half * 2048, f2T[h, 0:NKC], 8)
            p.wait("gpsimd", "LD", l)
            p.wait("gpsimd", "GD", g)
            d = wsl[:, half * 2048:half * 2048 + 8 * 128]
            gop(lambda g, d=d: g.tensor_single_scalar(out=d, in_=d, scalar=qcol[:, 53:54], op=ALU.mult))
            gop(lambda g, d=d: g.tensor_copy(out=wint[:, 0:1024], in_=d))
            gop(lambda g, d=d: g.tensor_copy(out=d, in_=wint[:, 0:1024]))
            gop(lambda g, d=d: g.tensor_scalar_max(d, d, -1.0))
            gop(lambda g, d=d: g.tensor_scalar_min(d, d, 1.0))
            gop(lambda g, d=d: g.tensor_single_scalar(out=d, in_=d, scalar=qcol[:, 51:52], op=ALU.mult))
            gq = gop(lambda g, half=half: g.tensor_single_scalar(
                out=cvb[:, half * TT:half * TT + TT],
                in_=cvb[:, half * TT:half * TT + TT],
                scalar=qcol[:, 47:48], op=ALU.mult))
            p.wait("tensor", "GE", gq)
            for o in range(NKC):
                top(lambda tn, o=o, h=h, half=half: tn.matmul(
                    pb[o][:], wsl[:, half * 2048 + o * 128:half * 2048 + (o + 1) * 128],
                    cvb[:, half * TT:half * TT + TT],
                    start=(h == 0), stop=(h == NHC - 1)))
            te_mark[h] = p.cur("TE")
        t = p.cur("TE")
        p.wait("vector", "TE", t)
        for o in range(NKC):
            p.wait("vector", "ST", p.cur("ST"))
            p.wait("gpsimd", "VE", p.cur("VE"))
            g = gld(t1[:], ao_sp[o * 128:(o + 1) * 128, tq * TT:(tq + 1) * TT])
            p.wait("sync", "VE", p.cur("VE"))
            l = ld(t2[:], xT[o * 128:(o + 1) * 128, tq * TT:(tq + 1) * TT])
            p.wait("vector", "GD", g)
            p.wait("vector", "LD", l)
            vop(lambda v, o=o: v.tensor_add(t0[:], pb[o][:], t1[:]))
            if o == 0 and tq == 0:
                vred(lambda v, o=o: v.tensor_copy(out=dbgs[0:1, 30:31], in_=pb[o][0:1, 0:1]))
                vred(lambda v: v.tensor_copy(out=dbgs[0:1, 31:32], in_=t1[0:1, 0:1]))
            v = vop(lambda v: v.tensor_add(t0[:], t0[:], t2[:]))
            p.wait("sync", "VE", v)
            p.inc("sync", lambda s, o=o, tq=tq: s.dma_start(
                out=outp[o * 128:(o + 1) * 128, tq * TT:(tq + 1) * TT], in_=t0[:]),
                "ST", 16)
        p.wait("gpsimd", "TE", p.cur("TE"))
    v = p.cur("VE")
    p.wait("sync", "VE", v)
    st_total = p.inc("sync", lambda s: s.dma_start(out=dbg, in_=dbgs[:]), "ST", 16)
    p.wait("gpsimd", "ST", st_total)

    blk = e(nc.Block())
    for eng_name, deco in (("sync", blk.sync), ("scalar", blk.scalar),
                           ("vector", blk.vector), ("gpsimd", blk.gpsimd),
                           ("tensor", blk.tensor)):
        ops = list(p.ops[eng_name])

        def make_body(ops):
            def body(eng):
                for op in ops:
                    op(eng)
            return body
        deco(make_body(ops))
    ctx.close()
    return nc



def _finish(nc, p, e, ctx, outp, dbg, dbgs, t0, outp_dummy=False):
    v = p.cur("VE")
    p.wait("sync", "VE", v)
    st = p.inc("sync", lambda s: s.dma_start(out=dbg, in_=dbgs[:]), "ST", 16)
    p.wait("gpsimd", "ST", st)
    blk = e(nc.Block())
    for eng_name, deco in (("sync", blk.sync), ("scalar", blk.scalar),
                           ("vector", blk.vector), ("gpsimd", blk.gpsimd),
                           ("tensor", blk.tensor)):
        ops = list(p.ops[eng_name])

        def make_body(ops):
            def body(eng):
                for op in ops:
                    op(eng)
            return body
        deco(make_body(ops))
    ctx.close()
    return nc

_NC_CACHE = None


def kernel(**inputs):
    global _NC_CACHE
    x = np.asarray(inputs["L0_attn_x"], np.float32)
    xf = np.ascontiguousarray(x.reshape(B * T, D))

    names = ["L0_attn_ln1_g", "L0_attn_ln1_b", "L0_attn_pw_b", "L0_attn_ln2_g",
             "L0_attn_ln2_b", "L0_attn_ln3_g", "L0_attn_ln3_b", "L0_ffn_ln_g",
             "L0_ffn_ln_b", "L0_attn_mlp_b2", "L0_attn_dw_b"]
    colsm = np.zeros((128, 96), np.float32)
    for idx, nm in enumerate(names):
        colsm[:, idx * 8:(idx + 1) * 8] = _col8(np.asarray(inputs[nm], np.float32))

    fb = np.asarray(inputs["L0_attn_f_b"], np.float32)
    fbc = np.zeros((128, 24), np.float32)
    for i in range(12):
        fbc[:, i] = fb[i * 128:(i + 1) * 128]
        fbc[:, 12 + i] = fb[EXP + i * 128:EXP + (i + 1) * 128]

    shared = {
        "dwt": np.ascontiguousarray(
            np.asarray(inputs["L0_attn_dw_w"], np.float32).reshape(D, 4)
            .reshape(NKC, 128, 4).transpose(1, 0, 2).reshape(128, NKC * 4)),
        "cols": np.ascontiguousarray(colsm),
        "fb_c": np.ascontiguousarray(fbc),
        "mb1_c": np.ascontiguousarray(
            np.asarray(inputs["L0_attn_mlp_b1"], np.float32).reshape(32, 128).T),
        "pwT": _tile4(np.ascontiguousarray(np.asarray(inputs["L0_attn_pw_w"], np.float32).T)),
        "fwT": _tile4(np.ascontiguousarray(np.asarray(inputs["L0_attn_f_w"], np.float32).T)),
        "dpT": _tile4(np.ascontiguousarray(np.asarray(inputs["L0_attn_dp_w"], np.float32).T)),
        "m1T": _tile4(np.ascontiguousarray(np.asarray(inputs["L0_attn_mlp_w1"], np.float32).T)),
        "m2T": _tile4(np.ascontiguousarray(np.asarray(inputs["L0_attn_mlp_w2"], np.float32).T)),
        "f1T": _tile4(np.ascontiguousarray(np.asarray(inputs["L0_ffn_w1"], np.float32).T)),
        "f2T": _tile4(np.ascontiguousarray(np.asarray(inputs["L0_ffn_w2"], np.float32).T)),
    }

    in_maps = []
    for c in range(NCORE):
        lo = c * TPC
        m = dict(shared)
        m["xT"] = np.ascontiguousarray(xf[lo:lo + TPC].T)
        xhp = np.zeros((128, 128), np.float32)
        if c % 2 == 1:
            hl = xf[lo - 3:lo].T.reshape(NKC, 128, 3).transpose(1, 0, 2).reshape(128, 24)
            xhp[:, 0:24] = hl
            hm, h0 = 1.0, 0.0
        else:
            hm, h0 = 0.0, 0.5
        m["xh"] = xhp
        pc = np.zeros((128, 4), np.float32)
        pc[:, 0] = hm
        pc[:, 1] = h0
        pc[c, 3] = 1.0
        if c % 2 == 1:
            pc[c - 1, 2] = 1.0
        m["percore"] = pc
        m["ident"] = np.eye(128, dtype=np.float32)
        oh = np.zeros((1, 8), np.float32)
        oh[0, c] = 1.0
        m["ohrow"] = oh
        pg = np.zeros((1, 16), np.float32)
        if c % 2 == 0:
            pg[0, c // 2] = 1.0
        else:
            pg[0, 8 + c // 2] = 1.0
        m["pgm"] = pg
        in_maps.append(m)

    if _NC_CACHE is None:
        _NC_CACHE = build_program()
    res = run_bass_kernel_spmd(_NC_CACHE, in_maps, core_ids=list(range(NCORE)))
    outs = [np.asarray(res.results[c]["out"]).T for c in range(NCORE)]
    return np.ascontiguousarray(np.concatenate(outs, axis=0).reshape(B, T, D))
